# version 01
# baseline (speedup 1.0000x reference)
import contextlib
import numpy as np
import ml_dtypes

import concourse.bass as bass
import concourse.mybir as mybir
from concourse.bass_utils import run_bass_kernel_spmd

DT = mybir.dt
F32 = DT.float32
BF16 = DT.bfloat16
AF = mybir.ActivationFunctionType
ALU = mybir.AluOpType
AX = mybir.AxisListType
BF = ml_dtypes.bfloat16

NCORES = 8
B, S, D = 4, 8192, 1024
POS = S // NCORES


class Ctr:
    __slots__ = ("h", "n")

    def __init__(self, h):
        self.h = h
        self.n = 0

    def inc(self, inst, k=1):
        inst.then_inc(self.h, k)
        self.n += k
        return self.n


class Prog:
    def __init__(self):
        self.nc = bass.Bass("TRN2", target_bir_lowering=False, debug=False)
        self.es = contextlib.ExitStack()
        self.blk = None

    def inp(self, name, shape, dtype=F32):
        return self.nc.dram_tensor(name, list(shape), dtype, kind="ExternalInput").ap()

    def out(self, name, shape, dtype=F32):
        return self.nc.dram_tensor(name, list(shape), dtype, kind="ExternalOutput").ap()

    def dram(self, name, shape, dtype=F32):
        return self.nc.dram_tensor(name, list(shape), dtype).ap()

    def sb(self, name, shape, dtype):
        return self.nc.alloc_sbuf_tensor(name, list(shape), dtype)

    def ps(self, name, shape, dtype=F32):
        return self.nc.alloc_psum_tensor(name, list(shape), dtype)

    def sem(self, name):
        return Ctr(self.es.enter_context(self.nc.semaphore(name)))

    def block(self):
        self.blk = self.es.enter_context(self.nc.Block())
        return self.blk

    def close(self):
        self.es.close()


def ident_np():
    return np.eye(128, dtype=BF)



def emit_weight_prep(p, blk, ident_sb, pst, jobs, tag, wld, wbf, wtr, scol, igw_max=2048):
    ld = [p.sem(f"{tag}ld0"), p.sem(f"{tag}ld1")]
    vr = p.sem(f"{tag}vr")
    tr = p.sem(f"{tag}tr")
    ev = p.sem(f"{tag}ev")
    st = [p.sem(f"{tag}st0"), p.sem(f"{tag}st1")]

    jobs = [j if len(j) == 6 else (*j, "panel") for j in jobs]
    tiles = []
    for j, (w, s_, wfm, no, ni, lay) in enumerate(jobs):
        igw = min(ni, igw_max)
        for oc in range(no // 128):
            for ig in range(ni // igw):
                tiles.append((j, oc, ig, igw))
    nt = len(tiles)
    nsub_of = [t[3] // 512 for t in tiles]
    sub_before = [0]
    for x in nsub_of:
        sub_before.append(sub_before[-1] + x)
    nsub = sub_before[nt]
    ld_tgt = [0] * nt
    job_last = {}
    for i, t in enumerate(tiles):
        job_last[t[0]] = i
    st_tgt = [0] * (nsub + 1)
    def st_totals(kmax):
        n0 = 64 * ((kmax + 1) // 2)
        n1 = 64 * (kmax // 2)
        return (n0, n1)
    job_tgt = [st_totals(sub_before[job_last[j] + 1]) for j in range(len(jobs))]

    @blk.sync
    def _(eng):
        for i, (j, oc, ig, igw) in enumerate(tiles):
            w, s_, wfm, no, ni, lay = jobs[j]
            if i >= 2:
                eng.wait_ge(vr.h, i - 1)
            ld[i % 2].inc(eng.dma_start(
                out=wld[i % 2][:, 0:igw],
                in_=w[oc * 128:(oc + 1) * 128, ig * igw:(ig + 1) * igw]), 16)
            if s_ is not None:
                ld[i % 2].inc(eng.dma_start(
                    out=scol[i % 2][:, 0:1],
                    in_=s_[oc * 128:(oc + 1) * 128].unsqueeze(1)), 16)
            ld_tgt[i] = ld[i % 2].n

    @blk.vector
    def _(eng):
        for i, (j, oc, ig, igw) in enumerate(tiles):
            w, s_, wfm, no, ni, lay = jobs[j]
            eng.wait_ge(ld[i % 2].h, ld_tgt[i])
            if i >= 2:
                eng.wait_ge(tr.h, sub_before[i - 1])
            if s_ is not None:
                vr.inc(eng.tensor_scalar_mul(
                    out=wbf[i % 2][:, 0:igw], in0=wld[i % 2][:, 0:igw],
                    scalar1=scol[i % 2][:, 0:1]))
            else:
                vr.inc(eng.tensor_copy(out=wbf[i % 2][:, 0:igw],
                                       in_=wld[i % 2][:, 0:igw]))

    @blk.tensor
    def _(eng):
        for i, (j, oc, ig, igw) in enumerate(tiles):
            eng.wait_ge(vr.h, i + 1)
            for sgi in range(igw // 512):
                k = sub_before[i] + sgi
                if k >= 2:
                    eng.wait_ge(ev.h, k - 1)
                for bi in range(4):
                    inst = eng.transpose(
                        pst[k % 2][:, bi * 128:(bi + 1) * 128],
                        wbf[i % 2][:, sgi * 512 + bi * 128:sgi * 512 + (bi + 1) * 128],
                        ident_sb[:, :])
                    if bi == 3:
                        tr.inc(inst)

    @blk.scalar
    def _(eng):
        for i, (j, oc, ig, igw) in enumerate(tiles):
            for sgi in range(igw // 512):
                k = sub_before[i] + sgi
                eng.wait_ge(tr.h, k + 1)
                if k >= 2:
                    eng.wait_ge(st[k % 2].h, 64 * ((k - 2) // 2 + 1))
                ev.inc(eng.copy(out=wtr[k % 2][:, :], in_=pst[k % 2][:, :]))

    @blk.gpsimd
    def _(eng):
        for i, (j, oc, ig, igw) in enumerate(tiles):
            w, s_, wfm, no, ni, lay = jobs[j]
            for sgi in range(igw // 512):
                k = sub_before[i] + sgi
                eng.wait_ge(ev.h, k + 1)
                for bi in range(4):
                    ic = (ig * igw + sgi * 512) // 128 + bi
                    if lay == "panel":
                        dst = wfm[oc, :, ic * 128:(ic + 1) * 128]
                    else:
                        dst = wfm[oc // 4, :, ic, (oc % 4) * 128:(oc % 4 + 1) * 128]
                    st[k % 2].inc(eng.dma_start(
                        out=dst,
                        in_=wtr[k % 2][:, bi * 128:(bi + 1) * 128]), 16)

    return dict(st=st, job_tgt=job_tgt, nt=nt, nsub=nsub, vr=vr, tr=tr, ev=ev)



def panels_np(w, s_, nic):
    no = w.shape[0] // 128
    ws = (w * s_[:, None]).astype(BF)
    out = np.zeros((no, 128, nic * 128), dtype=BF)
    for oc in range(no):
        for ic in range(nic):
            out[oc, :, ic * 128:(ic + 1) * 128] = ws[oc * 128:(oc + 1) * 128,
                                                     ic * 128:(ic + 1) * 128].T
    return out


def build_l1f(NT=POS * B, mode="full"):
    CHT = 512
    assert NT % CHT == 0
    NC = NT // CHT
    NTLE = CHT // 128
    NTILES = NT // 128

    p = Prog()
    x_in = p.inp("x", [NT, D])
    w1 = p.inp("w1_w", [4096, 1024]); s1 = p.inp("w1_s", [4096])
    w2 = p.inp("w2_w", [4096, 1024]); s2 = p.inp("w2_s", [4096])
    h1 = p.inp("h1_w", [4096, 4096]); sh1 = p.inp("h1_s", [4096])
    h2 = p.inp("h2_w", [4096, 4096]); sh2 = p.inp("h2_s", [4096])
    w3 = p.inp("w3_w", [1024, 4096]); s3 = p.inp("w3_s", [1024])
    idn = p.inp("ident", [128, 128], BF16)
    y_out = p.out("y", [NT, D])

    mk = p.dram if mode != "compute" else (lambda n, sh, dt=F32: p.inp(n, sh, dt))
    w1f = mk("w1f", [32, 128, 1024], BF16)
    w2f = mk("w2f", [32, 128, 1024], BF16)
    h1f = mk("h1f", [32, 128, 4096], BF16)
    h2f = mk("h2f", [32, 128, 4096], BF16)
    w3f = mk("w3f", [2, 128, 32, 512], BF16)
    w1d = p.out("w1d", [32, 128, 1024], BF16) if mode == "prep" else None

    ident_sb = p.sb("identb", [128, 128], BF16)

    xfm = p.sb("xfm", [128, 8 * CHT], BF16)
    ha = p.sb("ha", [128, 32 * CHT], BF16)
    hb = p.sb("hb", [128, 32 * CHT], BF16)
    silu_sb = [p.sb(f"silu{i}", [128, 512], F32) for i in range(2)]
    wld = [p.sb(f"wld{i}", [128, 2048], F32) for i in range(2)]
    wbf = [p.sb(f"wbf{i}", [128, 2048], BF16) for i in range(2)]
    wtr = [p.sb(f"wtr{i}", [128, 512], BF16) for i in range(2)]
    scol = [p.sb(f"sc{i}", [128, 1], F32) for i in range(2)]
    wstr = [p.sb(f"wstr{i}", [128, 4096], BF16) for i in range(2)]
    xr = [p.sb(f"xr{i}", [128, 1024], F32) for i in range(4)]
    yt = [p.sb(f"yt{i}", [128, 1024], F32) for i in range(4)]
    trash = p.sb("trash", [128, 1024], F32)
    stat = p.sb("stat", [128, 32], F32)
    epsln = p.sb("epsln", [128, 1], F32)

    pm = [p.ps(f"pm{i}", [128, 512], F32) for i in range(4)]
    pxt = [p.ps(f"pxt{i}", [128, 512], BF16) for i in range(2)]

    blk = p.block()

    ild = p.sem("ild")

    @blk.sync
    def _(eng):
        ild.inc(eng.dma_start(out=ident_sb[:, :], in_=idn), 16)

    @blk.tensor
    def _(eng):
        eng.wait_ge(ild.h, 16)

    if mode != "compute":
        prep = emit_weight_prep(
            p, blk, ident_sb, pxt,
            [(w1, s1, w1f, 4096, 1024), (w2, s2, w2f, 4096, 1024),
             (h1, sh1, h1f, 4096, 4096), (h2, sh2, h2f, 4096, 4096),
             (w3, s3, w3f, 1024, 4096, "grouped512")], "wp",
            wld, wbf, wtr, scol)
        st0, st1 = prep["st"]
        ST_W12, ST_H1, ST_H2, ST_W3 = (prep["job_tgt"][1], prep["job_tgt"][2],
                                       prep["job_tgt"][3], prep["job_tgt"][4])
        PREP_NT, PREP_NSUB = prep["nt"], prep["nsub"]
        prep_tr, prep_vr, prep_ev = prep["tr"], prep["vr"], prep["ev"]
    else:
        st0, st1 = p.sem("dst0"), p.sem("dst1")
        ST_W12 = ST_H1 = ST_H2 = ST_W3 = (0, 0)
        PREP_NT = PREP_NSUB = 0
        prep_tr, prep_vr, prep_ev = p.sem("dtr"), p.sem("dvr"), p.sem("dev")

    if mode == "prep":
        fin = p.sem("fin")

        @blk.sync
        def _(eng):
            eng.wait_ge(st0.h, ST_W3[0])
            eng.wait_ge(st1.h, ST_W3[1])
            fin.inc(eng.dma_start(out=w1d, in_=w1f), 16)
            eng.wait_ge(fin.h, 16)
        p.close()
        return p

    xl = [p.sem("xl0"), p.sem("xl1")]
    xv = p.sem("xv")
    xt = p.sem("xt")
    xe = p.sem("xe")
    wl = [p.sem("wl0"), p.sem("wl1")]
    idv = p.sem("idv")
    mmA1 = p.sem("mmA1")
    mmA2 = p.sem("mmA2")
    evA = p.sem("evA")
    evB = p.sem("evB")
    mmC = p.sem("mmC")
    evC = p.sem("evC")
    oev = p.sem("oev")
    xrl = [p.sem(f"xrl{i}") for i in range(4)]
    epv = p.sem("epv")
    sqd = p.sem("sqd")
    sqt = p.sem("sqt")
    epi = p.sem("epi")
    yst = [p.sem(f"yst{i}") for i in range(4)]

    sub = mode.split("-")[1] if "-" in mode else None
    PHASES = {None: ("a", "c1", "c2", "o"), "a": ("a",), "c1": ("a", "c1"),
              "c2": ("a", "c1", "c2"), "o": ("a", "c1", "c2", "o")}[sub]
    EPI = sub is None or sub == "o"
    units = []
    for ci in range(NC):
        for ph in ("a", "c1", "c2"):
            if ph in PHASES:
                for oc in range(32):
                    units.append((ph, ci, oc))
        if "o" in PHASES:
            for og in range(2):
                for icg in range(4):
                    units.append(("o", ci, og * 4 + icg))
    NU = len(units)
    wl_tgt = [0] * NU
    a_idx = {}
    u_idx = {}
    c_idx = {}
    na = nu2 = nc_ = 0
    for u, (ph, ci, oc) in enumerate(units):
        if ph == "a":
            a_idx[(ci, oc)] = na; na += 1
        else:
            u_idx[(ph, ci, oc)] = nu2; nu2 += 1
            if ph != "o":
                c_idx[(ph, ci, oc)] = nc_; nc_ += 1

    @blk.sync
    def _(eng):
        uu = 0
        for ci in range(NC):
            for tt in range(NTLE):
                gi = ci * NTLE + tt
                if gi < 2:
                    eng.wait_ge(prep_vr.h, PREP_NT)
                else:
                    eng.wait_ge(xv.h, gi - 1)
                xl[gi % 2].inc(eng.dma_start(
                    out=wld[gi % 2][:, 0:1024],
                    in_=x_in[gi * 128:(gi + 1) * 128, :]), 16)
            for tt in (range(NTLE) if EPI else []):
                gi = ci * NTLE + tt
                if gi >= 4:
                    eng.wait_ge(epi.h, gi - 3)
                xrl[gi % 4].inc(eng.dma_start(
                    out=xr[gi % 4][:, :],
                    in_=x_in[gi * 128:(gi + 1) * 128, :]), 16)
            while uu < NU and units[uu][1] == ci:
                ph, _ci, oc = units[uu]
                if uu >= 2:
                    pph, pci, poc = units[uu - 2]
                    if pph == "a":
                        eng.wait_ge(mmA2.h, a_idx[(pci, poc)] + 1)
                    else:
                        eng.wait_ge(mmC.h, u_idx[(pph, pci, poc)] + 1)
                if (ph, ci, oc) == ("a", 0, 0):
                    eng.wait_ge(st0.h, ST_W12[0]); eng.wait_ge(st1.h, ST_W12[1])
                if (ph, ci, oc) == ("c1", 0, 0):
                    eng.wait_ge(st0.h, ST_H1[0]); eng.wait_ge(st1.h, ST_H1[1])
                if (ph, ci, oc) == ("c2", 0, 0):
                    eng.wait_ge(st0.h, ST_H2[0]); eng.wait_ge(st1.h, ST_H2[1])
                if (ph, ci, oc) == ("o", 0, 0):
                    eng.wait_ge(st0.h, ST_W3[0]); eng.wait_ge(st1.h, ST_W3[1])
                if ph == "a":
                    wl[uu % 2].inc(eng.dma_start(out=wstr[uu % 2][:, 0:1024],
                                                 in_=w1f[oc, :, :]), 16)
                    wl[uu % 2].inc(eng.dma_start(out=wstr[uu % 2][:, 1024:2048],
                                                 in_=w2f[oc, :, :]), 16)
                elif ph == "c1":
                    wl[uu % 2].inc(eng.dma_start(out=wstr[uu % 2][:, :],
                                                 in_=h1f[oc, :, :]), 16)
                elif ph == "c2":
                    wl[uu % 2].inc(eng.dma_start(out=wstr[uu % 2][:, :],
                                                 in_=h2f[oc, :, :]), 16)
                else:
                    og, icg = oc // 4, oc % 4
                    wl[uu % 2].inc(eng.dma_start(
                        out=wstr[uu % 2][:, :],
                        in_=w3f[og, :, icg * 8:(icg + 1) * 8, :]), 16)
                wl_tgt[uu] = wl[uu % 2].n
                uu += 1

    @blk.vector
    def _(eng):
        eng.wait_ge(ild.h, 16)
        idv.inc(eng.memset(epsln[:, :], 1e-5))
        for ci in range(NC):
            for tt in range(NTLE):
                gi = ci * NTLE + tt
                eng.wait_ge(xl[gi % 2].h, 16 * (gi // 2 + 1))
                if gi < 2:
                    eng.wait_ge(prep_tr.h, PREP_NSUB)
                else:
                    eng.wait_ge(xt.h, 2 * (gi - 1))
                xv.inc(eng.tensor_copy(out=wbf[gi % 2][:, 0:1024],
                                       in_=wld[gi % 2][:, 0:1024]))
            for oc in (range(32) if "a" in PHASES else []):
                ai = a_idx[(ci, oc)]
                eng.wait_ge(mmA2.h, ai + 1)
                eng.wait_ge(evA.h, ai + 1)
                evB.inc(eng.scalar_tensor_tensor(
                    out=ha[:, oc * CHT:(oc + 1) * CHT],
                    in0=pm[1][:, 0:CHT], scalar=1.0,
                    in1=silu_sb[ai % 2][:, 0:CHT],
                    op0=ALU.mult, op1=ALU.mult))
            for og in (range(2) if "o" in PHASES else []):
                eng.wait_ge(mmC.h, u_idx[("o", ci, og * 4 + 3)] + 1)
                for tt in range(NTLE):
                    gi = ci * NTLE + tt
                    eng.wait_ge(xrl[gi % 4].h, 16 * (gi // 4 + 1))
                    if gi >= 4:
                        eng.wait_ge(yst[gi % 4].h, 16 * ((gi - 4) // 4 + 1))
                    oev.inc(eng.scalar_tensor_tensor(
                        out=yt[gi % 4][:, og * 512:(og + 1) * 512],
                        in0=pm[tt][:, 0:512], scalar=1.0,
                        in1=xr[gi % 4][:, og * 512:(og + 1) * 512],
                        op0=ALU.mult, op1=ALU.add))
            for tt in (range(NTLE) if EPI else []):
                gi = ci * NTLE + tt
                eng.wait_ge(oev.h, 8 * ci + 4 + tt + 1)
                c0 = (gi % 4) * 8
                e0 = 5 * gi
                epv.inc(eng.reduce_sum(out=stat[:, c0:c0 + 1], in_=yt[gi % 4][:, :],
                                       axis=AX.X))
                eng.wait_ge(epv.h, e0 + 1)
                epv.inc(eng.tensor_scalar_mul(out=stat[:, c0 + 2:c0 + 3],
                                              in0=stat[:, c0:c0 + 1],
                                              scalar1=1.0 / 1024.0))
                eng.wait_ge(epv.h, e0 + 2)
                epv.inc(eng.tensor_mul(out=stat[:, c0 + 3:c0 + 4],
                                       in0=stat[:, c0 + 2:c0 + 3],
                                       in1=stat[:, c0 + 2:c0 + 3]))
                eng.wait_ge(sqd.h, gi + 1)
                eng.wait_ge(epv.h, e0 + 3)
                epv.inc(eng.tensor_scalar(
                    out=stat[:, c0 + 4:c0 + 5], in0=stat[:, c0 + 1:c0 + 2],
                    scalar1=1.0 / 1024.0, scalar2=stat[:, c0 + 3:c0 + 4],
                    op0=ALU.mult, op1=ALU.subtract))
                eng.wait_ge(sqt.h, gi + 1)
                epv.inc(eng.reciprocal(out=stat[:, c0 + 6:c0 + 7],
                                       in_=stat[:, c0 + 5:c0 + 6]))
                eng.wait_ge(epv.h, e0 + 5)
                epi.inc(eng.tensor_scalar(
                    out=yt[gi % 4][:, :], in0=yt[gi % 4][:, :],
                    scalar1=stat[:, c0 + 2:c0 + 3], scalar2=stat[:, c0 + 6:c0 + 7],
                    op0=ALU.subtract, op1=ALU.mult))

    @blk.tensor
    def _(eng):
        uu = 0
        for ci in range(NC):
            for tt in range(NTLE):
                gi = ci * NTLE + tt
                eng.wait_ge(xv.h, gi + 1)
                for half in range(2):
                    k = gi * 2 + half
                    if k < 2:
                        eng.wait_ge(prep_ev.h, PREP_NSUB)
                    else:
                        eng.wait_ge(xe.h, k - 1)
                    for bi in range(4):
                        inst = eng.transpose(
                            pxt[k % 2][:, bi * 128:(bi + 1) * 128],
                            wbf[gi % 2][:, half * 512 + bi * 128
                                        :half * 512 + (bi + 1) * 128],
                            ident_sb[:, :])
                        if bi == 3:
                            xt.inc(inst)
            while uu < NU and units[uu][1] == ci:
                ph, _ci, oc = units[uu]
                eng.wait_ge(wl[uu % 2].h, wl_tgt[uu])
                if ph == "a":
                    ai = a_idx[(ci, oc)]
                    eng.wait_ge(xe.h, 8 * (ci + 1))
                    if oc == 0 and "o" in PHASES and ci >= 1:
                        eng.wait_ge(oev.h, 8 * ci)
                    if ai >= 1:
                        eng.wait_ge(evA.h, ai)
                    for ic in range(8):
                        inst = eng.matmul(pm[0][:, 0:CHT],
                                          wstr[uu % 2][:, ic * 128:(ic + 1) * 128],
                                          xfm[:, ic * CHT:(ic + 1) * CHT],
                                          start=(ic == 0), stop=(ic == 7))
                        if ic == 7:
                            mmA1.inc(inst)
                    if ai >= 1:
                        eng.wait_ge(evB.h, ai)
                    for ic in range(8):
                        inst = eng.matmul(pm[1][:, 0:CHT],
                                          wstr[uu % 2][:, 1024 + ic * 128
                                                       :1024 + (ic + 1) * 128],
                                          xfm[:, ic * CHT:(ic + 1) * CHT],
                                          start=(ic == 0), stop=(ic == 7))
                        if ic == 7:
                            mmA2.inc(inst)
                elif ph in ("c1", "c2"):
                    kc = c_idx[(ph, ci, oc)]
                    src = ha if ph == "c1" else hb
                    if oc == 0 and ph == "c1":
                        eng.wait_ge(evB.h, 32 * (ci + 1))
                        if "o" in PHASES and ci >= 1:
                            eng.wait_ge(oev.h, 8 * ci)
                    if oc == 0 and ph == "c2":
                        eng.wait_ge(evC.h, c_idx[("c2", ci, 0)])
                    if kc >= 2:
                        eng.wait_ge(evC.h, kc - 1)
                    for ic in range(32):
                        inst = eng.matmul(pm[2 + kc % 2][:, 0:CHT],
                                          wstr[uu % 2][:, ic * 128:(ic + 1) * 128],
                                          src[:, ic * CHT:(ic + 1) * CHT],
                                          start=(ic == 0), stop=(ic == 31))
                        if ic == 31:
                            mmC.inc(inst)
                else:
                    og, icg = oc // 4, oc % 4
                    if icg == 0:
                        if og == 0:
                            eng.wait_ge(evC.h, 64 * (ci + 1))
                            if ci >= 1:
                                eng.wait_ge(oev.h, 8 * ci)
                        else:
                            eng.wait_ge(oev.h, 8 * ci + 4)
                    for tt in range(NTLE):
                        for j in range(8):
                            ic = icg * 8 + j
                            inst = eng.matmul(
                                pm[tt][:, 0:512],
                                ha[:, ic * CHT + tt * 128:ic * CHT + (tt + 1) * 128],
                                wstr[uu % 2][:, j * 512:(j + 1) * 512],
                                start=(icg == 0 and j == 0),
                                stop=(icg == 3 and j == 7))
                            if tt == NTLE - 1 and j == 7:
                                mmC.inc(inst)
                uu += 1

    @blk.scalar
    def _(eng):
        for ci in range(NC):
            for tt in range(NTLE):
                gi = ci * NTLE + tt
                for half in range(2):
                    k = gi * 2 + half
                    eng.wait_ge(xt.h, k + 1)
                    if ci >= 1:
                        eng.wait_ge(mmA2.h, 32 * ci)
                    ic0 = half * 4
                    xe.inc(eng.copy(
                        out=xfm[:, :].rearrange(
                            "p (c t) -> p c t", c=8)[:, ic0:ic0 + 4,
                                                     tt * 128:(tt + 1) * 128],
                        in_=pxt[k % 2][:, :].rearrange(
                            "p (c t) -> p c t", c=4)))
            for oc in (range(32) if "a" in PHASES else []):
                ai = a_idx[(ci, oc)]
                eng.wait_ge(mmA1.h, ai + 1)
                if ai >= 2:
                    eng.wait_ge(evB.h, ai - 1)
                evA.inc(eng.activation(out=silu_sb[ai % 2][:, 0:CHT],
                                       in_=pm[0][:, 0:CHT], func=AF.Silu))
            for phi, ph in [x for x in ((0, "c1"), (1, "c2")) if x[1] in PHASES]:
                dst = hb if ph == "c1" else ha
                for oc in range(32):
                    kc = c_idx[(ph, ci, oc)]
                    eng.wait_ge(mmC.h, u_idx[(ph, ci, oc)] + 1)
                    evC.inc(eng.activation(out=dst[:, oc * CHT:(oc + 1) * CHT],
                                           in_=pm[2 + kc % 2][:, 0:CHT],
                                           func=AF.Gelu))
            for tt in (range(NTLE) if EPI else []):
                gi = ci * NTLE + tt
                c0 = (gi % 4) * 8
                if gi == 0:
                    eng.wait_ge(idv.h, 1)
                eng.wait_ge(oev.h, 8 * (gi // NTLE) + 4 + (gi % NTLE) + 1)
                sqd.inc(eng.activation(out=trash[:, :], in_=yt[gi % 4][:, :],
                                       func=AF.Square,
                                       accum_out=stat[:, c0 + 1:c0 + 2]))
                eng.wait_ge(epv.h, 5 * gi + 4)
                sqt.inc(eng.activation(out=stat[:, c0 + 5:c0 + 6],
                                       in_=stat[:, c0 + 4:c0 + 5],
                                       func=AF.Sqrt, bias=epsln[:, 0:1]))

    if EPI:
        @blk.gpsimd
        def _(eng):
            for gi in range(NTILES):
                eng.wait_ge(epi.h, gi + 1)
                yst[gi % 4].inc(eng.dma_start(out=y_out[gi * 128:(gi + 1) * 128, :],
                                              in_=yt[gi % 4][:, :]), 16)
            for q in range(4):
                eng.wait_ge(yst[q].h, 16 * ((NTILES - q + 3) // 4))
    else:
        dump = p.out("dump", [128, 32 * CHT], BF16)

        @blk.gpsimd
        def _(eng):
            if sub == "a":
                eng.wait_ge(evB.h, 32 * NC)
                src = ha
            elif sub == "c1":
                eng.wait_ge(evC.h, c_idx[("c1", NC - 1, 31)] + 1)
                src = hb
            else:
                eng.wait_ge(evC.h, c_idx[("c2", NC - 1, 31)] + 1)
                src = ha
            yst[0].inc(eng.dma_start(out=dump, in_=src[:, :]), 16)
            eng.wait_ge(yst[0].h, 16)

    p.close()
    return p



def _l1f_in_maps(inputs, x_full):
    idn = ident_np()
    maps = []
    for c in range(NCORES):
        m = {
            "x": np.ascontiguousarray(x_full[c * (POS * B):(c + 1) * (POS * B)]),
            "w1_w": inputs["L1_ffn_w1_w"], "w1_s": inputs["L1_ffn_w1_s"],
            "w2_w": inputs["L1_ffn_w2_w"], "w2_s": inputs["L1_ffn_w2_s"],
            "h1_w": inputs["L1_ffn_h1_w"], "h1_s": inputs["L1_ffn_h1_s"],
            "h2_w": inputs["L1_ffn_h2_w"], "h2_s": inputs["L1_ffn_h2_s"],
            "w3_w": inputs["L1_ffn_w3_w"], "w3_s": inputs["L1_ffn_w3_s"],
            "ident": idn,
        }
        maps.append(m)
    return maps


def tokens_of_core(x_bsd):
    xr = x_bsd.reshape(B, NCORES, POS, D).transpose(1, 0, 2, 3)
    return np.ascontiguousarray(xr.reshape(NCORES, B * POS, D))


def tokens_to_bsd(y_cores):
    y = np.asarray(y_cores).reshape(NCORES, B, POS, D).transpose(1, 0, 2, 3)
    return np.ascontiguousarray(y.reshape(B, S, D))


def run_l1f(inputs, x_bsd):
    p = build_l1f()
    xc = tokens_of_core(x_bsd)
    maps = _l1f_in_maps(inputs, xc.reshape(NCORES * B * POS, D))
    res = run_bass_kernel_spmd(p.nc, maps, core_ids=list(range(NCORES)))
    y = np.stack([res.results[c]["y"] for c in range(NCORES)])
    return tokens_to_bsd(y)



def build_l1a(NT=POS * B, NB=4):
    PP = NT // NB
    NG = PP // 128
    p = Prog()
    x_in = p.inp("x", [NT, D])
    wq = p.inp("wq", [1024, 1024]); bq = p.inp("bq", [1024])
    wk = p.inp("wk", [1024, 1024]); bk = p.inp("bk", [1024])
    wv = p.inp("wv", [1024, 1024]); bv = p.inp("bv", [1024])
    wo = p.inp("wo", [1024, 1024]); bo = p.inp("bo", [1024])
    w1 = p.inp("w1", [4096, 1024]); b1 = p.inp("b1", [4096])
    w2 = p.inp("w2", [1024, 4096]); b2 = p.inp("b2", [1024])
    g1 = p.inp("g1", [1024]); beta1 = p.inp("beta1", [1024])
    g2 = p.inp("g2", [1024]); beta2 = p.inp("beta2", [1024])
    idn = p.inp("ident", [128, 128], BF16)
    y_out = p.out("y", [NT, D])

    wqf = p.dram("wqf", [2, 128, 8, 512], BF16)
    wkf = p.dram("wkf", [2, 128, 8, 512], BF16)
    wvf = p.dram("wvf", [2, 128, 8, 512], BF16)
    wof = p.dram("wof", [2, 128, 8, 512], BF16)
    w1f = p.dram("w1f", [8, 128, 8, 512], BF16)
    w2f = p.dram("w2f", [2, 128, 32, 512], BF16)

    ident_sb = p.sb("identb", [128, 128], BF16)
    big = p.sb("big", [128, 16384], F32)
    xres = [p.sb(f"xres{i}", [128, 1024], F32) for i in range(4)]
    x1b = [p.sb(f"x1b{i}", [128, 1024], F32) for i in range(4)]
    ffb = [p.sb(f"ffb{i}", [128, 1024], F32) for i in range(4)]
    xfm = [p.sb(f"xfmt{i}", [128, 8 * 128], BF16) for i in range(4)]
    ofm = xfm
    x1fm = xfm
    hfm = [p.sb(f"hfmt{i}", [128, 32 * 128], BF16) for i in range(2)]
    hbf = [p.sb("hbft0", [128, 4096], BF16)]
    wld = [p.sb(f"wldt{i}", [128, 512], F32) for i in range(2)]
    cst = [wld[0][:, :].bitcast(BF16), wld[1][:, :].bitcast(BF16)]
    wbf2 = [p.sb(f"wbft{i}", [128, 512], BF16) for i in range(2)]
    wtr = [p.sb(f"wtrt{i}", [128, 512], BF16) for i in range(2)]
    scol = [p.sb(f"sct{i}", [128, 1], F32) for i in range(2)]
    wstr = [p.sb(f"wstt{i}", [128, 4096], BF16) for i in range(2)]
    trash = [p.sb(f"trsh{i}", [128, 1024], F32) for i in range(2)]
    tmpv = p.sb("tmpv", [128, 1024], F32)
    tmpp = p.sb("tmpp", [128, 1024], F32)
    bias_f32 = [tmpv[0:1, 0:512], tmpv[0:1, 512:1024]]
    gb_stage = [tmpp[0:1, 0:512], tmpp[0:1, 512:1024]]
    bias_stage = [trash[0][0:1, :].bitcast(BF16)[0:1, 0:512],
                  trash[0][0:1, :].bitcast(BF16)[0:1, 512:1024]]
    dots = p.sb("dots", [128, 256], F32)
    dens = p.sb("dens", [128, 256], F32)
    sqs = dots
    eee = dens
    smx = p.sb("smx", [128, 64], F32)
    ssum = p.sb("ssum", [128, 64], F32)
    qn_sb = p.sb("qn_sb", [128, 64], F32)
    kn_sb = p.sb("kn_sb", [128, 64], F32)
    stat = p.sb("statA", [128, 64], F32)
    gbt = p.sb("gbt", [128, 4096], F32)
    bias_dram = p.dram("biasd", [9216], BF16)
    bias_sb = [p.sb(f"bias_sb{i}", [1, 512], BF16) for i in range(2)]
    bld_rec = []
    ones_bf = p.sb("ones_bf", [1, 128], BF16)
    ones_f32 = p.sb("ones_f32", [1, 128], F32)
    eps8 = p.sb("eps8", [128, 1], F32)
    epsl = p.sb("epsl", [128, 1], F32)

    pm = [p.ps(f"pmA{i}", [128, 512], F32) for i in range(4)]
    pgb = p.ps("pgb", [128, 512], F32)
    pxt = [p.ps(f"pxtA{i}", [128, 512], BF16) for i in range(2)]

    blk = p.block()

    ild = p.sem("ild")
    bld = [p.sem("bld0"), p.sem("bld1")]
    cns = p.sem("cns")
    gbs = p.sem("gbs")
    gbe = p.sem("gbe")

    bias_srcs = ([(bq, 0, i * 512) for i in range(2)]
                 + [(bk, 0, 1024 + i * 512) for i in range(2)][0:0])
    bias_items = []
    for vec, base in ((bq, 0), (bk, 1024), (bv, 2048), (bo, 3072)):
        for i in range(2):
            bias_items.append((vec, i * 512, base + i * 512))
    for i in range(8):
        bias_items.append((b1, i * 512, 4096 + i * 512))
    for i in range(2):
        bias_items.append((b2, i * 512, 8192 + i * 512))
    gb_items = []
    for vec, gcol in ((g1, 0), (beta1, 1024), (g2, 2048), (beta2, 3072)):
        for i in range(2):
            gb_items.append((vec, i * 512, gcol + i * 512))

    @blk.sync
    def _(eng):
        ild.inc(eng.dma_start(out=ident_sb[:, :], in_=idn), 16)
        for ii, (vec, off, dst) in enumerate(bias_items + gb_items):
            if ii >= 2:
                eng.wait_ge(cns.h, ii - 1)
            bld[ii % 2].inc(eng.dma_start(out=bias_f32[ii % 2][0:1, :],
                                          in_=vec[off:off + 512].unsqueeze(0)), 16)
            bld_rec.append(bld[ii % 2].n)

    bst = [p.sem("bst0"), p.sem("bst1")]
    bst_rec = []

    @blk.vector
    def _(eng):
        eng.wait_ge(ild.h, 16)
        eng.memset(eps8[:, :], 1e-8)
        eng.memset(epsl[:, :], 1e-5)
        eng.memset(ones_bf[:, :], 1.0)
        eng.memset(ones_f32[:, :], 1.0)
        for ii, (vec, off, dst) in enumerate(bias_items):
            eng.wait_ge(bld[ii % 2].h, bld_rec[ii])
            if ii >= 2:
                eng.wait_ge(bst[ii % 2].h, 16 * (ii // 2))
            cns.inc(eng.tensor_copy(out=bias_stage[ii % 2][0:1, :],
                                    in_=bias_f32[ii % 2][0:1, :]))
        for jj in range(len(gb_items)):
            ii = len(bias_items) + jj
            eng.wait_ge(bld[ii % 2].h, bld_rec[ii])
            eng.wait_ge(gbe.h, max(0, jj - 1))
            cns.inc(eng.tensor_copy(out=gb_stage[jj % 2][0:1, :],
                                    in_=bias_f32[ii % 2][0:1, :]))

    @blk.gpsimd
    def _(eng):
        for ii, (vec, off, dst) in enumerate(bias_items):
            eng.wait_ge(cns.h, ii + 1)
            bst[ii % 2].inc(eng.dma_start(
                out=bias_dram[dst:dst + 512].unsqueeze(0),
                in_=bias_stage[ii % 2][0:1, :]), 16)
    BST_ALL = (bst[0].n, bst[1].n)
    CNS_ALL = cns.n

    @blk.tensor
    def _(eng):
        eng.wait_ge(ild.h, 16)
        for jj, (vec, off, gcol) in enumerate(gb_items):
            eng.wait_ge(cns.h, len(bias_items) + jj + 1)
            if jj >= 1:
                eng.wait_ge(gbe.h, jj)
            gbs.inc(eng.matmul(pgb[:, :], ones_f32[0:1, :],
                               gb_stage[jj % 2][0:1, :],
                               start=True, stop=True))

    @blk.scalar
    def _(eng):
        for jj, (vec, off, gcol) in enumerate(gb_items):
            eng.wait_ge(gbs.h, jj + 1)
            gbe.inc(eng.copy(out=gbt[:, gcol:gcol + 512], in_=pgb[:, :]))

    prep = emit_weight_prep(
        p, blk, ident_sb, pxt,
        [(wq, None, wqf, 1024, 1024, "grouped512"),
         (wk, None, wkf, 1024, 1024, "grouped512"),
         (wv, None, wvf, 1024, 1024, "grouped512"),
         (wo, None, wof, 1024, 1024, "grouped512"),
         (w1, None, w1f, 4096, 1024, "grouped512"),
         (w2, None, w2f, 1024, 4096, "grouped512")], "wa",
        wld, wbf2, wtr, scol, igw_max=512)
    st0, st1 = prep["st"]
    ST_QKV = prep["job_tgt"][2]
    ST_WO = prep["job_tgt"][3]
    ST_W1 = prep["job_tgt"][4]
    ST_W2 = prep["job_tgt"][5]
    PREP_NSUB = prep["nsub"]
    PREP_NT = prep["nt"]
    prep_ev = prep["ev"]
    prep_vr = prep["vr"]
    prep_tr = prep["tr"]

    xl = [p.sem(f"xlA{i}") for i in range(4)]
    wl = [p.sem(f"wlA{i}") for i in range(2)]
    mmb = p.sem("mmb")
    trc = p.sem("trc")
    tre = p.sem("tre")
    cvD = p.sem("cvD")
    sqD = p.sem("sqD")
    snD = p.sem("snD")
    cvc = p.sem("cvc")
    aev = p.sem("aev")
    chv = p.sem("chv")
    chp = p.sem("chp")
    chs = p.sem("chs")
    qkr = p.sem("qkr")
    hsq = p.sem("hsq")
    rel = p.sem("rel")
    epi = p.sem("epi")
    yst = [p.sem(f"ystA{i}") for i in range(2)]

    UPG = 32
    wl_rec = {}
    _incs = [4] * 16 + [2] * 16
    mmb_prefix = []
    _c = 0
    for _i in _incs:
        _c += _i
        mmb_prefix.append(_c)
    D_MMB, D_UC, D_TRC, D_CVC, D_AEV = 96, 32, 56, 12, 72
    D_CHP = 28

    def unit_list():
        us = []
        for proj, wf, stj in (("q", wqf, ST_QKV), ("k", wkf, ST_QKV),
                              ("v", wvf, ST_QKV)):
            for og in range(2):
                us.append((proj, og, 0, wf, stj))
        for og in range(2):
            us.append(("wo", og, 0, wof, ST_WO))
        for og in range(8):
            us.append(("w1", og, 0, w1f, ST_W1))
        for bp in range(2):
            for og in range(2):
                for icg in range(4):
                    us.append(("w2", og, icg * 10 + bp, w2f, ST_W2))
        return us

    US = unit_list()

    def qoff(b): return b * 1024
    def koff(b): return 4096 + b * 1024
    def voff(b): return 8192 + b * 1024
    def ooff(b): return 12288 + b * 1024

    @blk.sync
    def _(eng):
        for g in range(NG):
            for b in range(4):
                gi = b * NG + g
                if g >= 1:
                    eng.wait_ge(epi.h, 4 * (g - 1) + b + 1)
                xl[b].inc(eng.dma_start(
                    out=xres[b][:, :],
                    in_=x_in[gi * 128:(gi + 1) * 128, :]), 16)
            for ui, (proj, og, icg, wf, stj) in enumerate(US):
                U = g * UPG + ui
                if U >= 2:
                    pg, pui = (U - 2) // UPG, (U - 2) % UPG
                    eng.wait_ge(mmb.h, pg * D_MMB + mmb_prefix[pui])
                if U == 0:
                    eng.wait_ge(bst[0].h, BST_ALL[0])
                    eng.wait_ge(bst[1].h, BST_ALL[1])
                if g == 0 and ui in (0, 6, 8, 16):
                    eng.wait_ge(st0.h, stj[0])
                    eng.wait_ge(st1.h, stj[1])
                if proj == "w2":
                    icg_r = icg // 10
                    src = wf[og, :, icg_r * 8:(icg_r + 1) * 8, :]
                else:
                    src = wf[og, :, :, :]
                wl[U % 2].inc(eng.dma_start(out=wstr[U % 2][:, :], in_=src), 16)
                boff = {"q": 0, "k": 1024, "v": 2048, "wo": 3072,
                        "w1": 4096, "w2": 8192}[proj] + og * 512
                if proj != "w2" or icg // 10 == 3:
                    wl[U % 2].inc(eng.dma_start(
                        out=bias_sb[U % 2][0:1, :],
                        in_=bias_dram[boff:boff + 512].unsqueeze(0)), 16)
                wl_rec[U] = wl[U % 2].n

    mmb_rec = {}
    trc_rec = {}

    @blk.tensor
    def _(eng):
        for g in range(NG):
            trc_base = g * D_TRC
            cvc_base = g * D_CVC
            aev_base = g * D_AEV
            mmb_base = g * D_MMB

            def do_transposes(src_of, nhalf, koffs, cwait_of):
                for b in range(4):
                    eng.wait_ge(cvc.h, cwait_of(b))
                    for half in range(nhalf):
                        k = trc_base + koffs + b * nhalf + half
                        if k >= 2:
                            eng.wait_ge(tre.h, k - 1)
                        tile, cb = src_of(b, half)
                        for bi in range(4):
                            trc.inc(eng.transpose(
                                pxt[k % 2][:, bi * 128:(bi + 1) * 128],
                                tile[:, cb + bi * 128:cb + (bi + 1) * 128],
                                ident_sb[:, :]))
                        trc_rec[(g, koffs, b, half)] = trc.n

            if g == 0:
                eng.wait_ge(prep_ev.h, PREP_NSUB)
            do_transposes(lambda b, h: (cst[b % 2], h * 512), 2, 0,
                          lambda b: cvc_base + b + 1)
            for ui, (proj, og, icg, wf, stj) in enumerate(US):
                U = g * UPG + ui
                eng.wait_ge(wl[U % 2].h, wl_rec[U])
                if proj in ("q", "k", "v"):
                    eng.wait_ge(tre.h, trc_base + 8)
                    for b in range(4):
                        if ui >= 1:
                            eng.wait_ge(aev.h, aev_base + (ui - 1) * 4 + b + 1)
                        elif g >= 1:
                            eng.wait_ge(aev.h, aev_base)
                        for ic in range(8):
                            inst = eng.matmul(
                                pm[b][:, :],
                                xfm[b][:, ic * 128:(ic + 1) * 128],
                                wstr[U % 2][:, ic * 512:(ic + 1) * 512],
                                start=(ic == 0), stop=False)
                        inst = eng.matmul(pm[b][:, :], ones_bf[0:1, :],
                                          bias_sb[U % 2][0:1, :],
                                          start=False, stop=True)
                        mmb.inc(inst)
                        mmb_rec[(g, ui, b)] = mmb.n
                elif proj == "wo":
                    if og == 0:
                        do_transposes(lambda b, h: (cst[b % 2], h * 512), 2, 8,
                                      lambda b: cvc_base + 4 + b + 1)
                    for b in range(4):
                        eng.wait_ge(tre.h, trc_base + 16)
                        eng.wait_ge(aev.h, aev_base + (ui - 1) * 4 + b + 1)
                        for ic in range(8):
                            inst = eng.matmul(
                                pm[b][:, :],
                                ofm[b][:, ic * 128:(ic + 1) * 128],
                                wstr[U % 2][:, ic * 512:(ic + 1) * 512],
                                start=(ic == 0), stop=False)
                        inst = eng.matmul(pm[b][:, :], ones_bf[0:1, :],
                                          bias_sb[U % 2][0:1, :],
                                          start=False, stop=True)
                        mmb.inc(inst)
                        mmb_rec[(g, ui, b)] = mmb.n
                elif proj == "w1":
                    if og == 0:
                        do_transposes(lambda b, h: (cst[b % 2], h * 512), 2, 16,
                                      lambda b: cvc_base + 8 + b + 1)
                    for b in range(4):
                        eng.wait_ge(tre.h, trc_base + 24)
                        eng.wait_ge(aev.h, aev_base + (ui - 1) * 4 + b + 1)
                        for ic in range(8):
                            inst = eng.matmul(
                                pm[b][:, :],
                                x1fm[b][:, ic * 128:(ic + 1) * 128],
                                wstr[U % 2][:, ic * 512:(ic + 1) * 512],
                                start=(ic == 0), stop=False)
                        inst = eng.matmul(pm[b][:, :], ones_bf[0:1, :],
                                          bias_sb[U % 2][0:1, :],
                                          start=False, stop=True)
                        mmb.inc(inst)
                        mmb_rec[(g, ui, b)] = mmb.n
                else:
                    icg_r, bp = icg // 10, icg % 10
                    if og == 0 and icg_r == 0:
                        for b in (2 * bp, 2 * bp + 1):
                            eng.wait_ge(rel.h, g * 4 + b + 1)
                            for half in range(8):
                                k = trc_base + 24 + bp * 16 + (b - 2 * bp) * 8 + half
                                eng.wait_ge(tre.h, k - 1)
                                for bi in range(4):
                                    trc.inc(eng.transpose(
                                        pxt[k % 2][:, bi * 128:(bi + 1) * 128],
                                        hbf[0][:, half * 512 + bi * 128
                                               :half * 512 + (bi + 1) * 128],
                                        ident_sb[:, :]))
                    for b in (2 * bp, 2 * bp + 1):
                        b0 = b - 2 * bp
                        if icg_r == 0:
                            eng.wait_ge(tre.h, trc_base + 24 + (bp + 1) * 16)
                            if og == 0 and bp == 0:
                                eng.wait_ge(aev.h, aev_base + 60 + b + 1)
                            elif og == 0 and bp == 1:
                                eng.wait_ge(aev.h, aev_base + 68)
                                eng.wait_ge(aev.h, aev_base + 60 + b + 1)
                            else:
                                eng.wait_ge(aev.h,
                                            aev_base + 64 + bp * 4 + b0 + 1)
                        inst = None
                        for j in range(8):
                            ic = icg_r * 8 + j
                            inst = eng.matmul(
                                pm[b0][:, :],
                                hfm[b % 2][:, ic * 128:(ic + 1) * 128],
                                wstr[U % 2][:, j * 512:(j + 1) * 512],
                                start=(icg_r == 0 and j == 0), stop=False)
                        if icg_r == 3:
                            inst = eng.matmul(pm[b0][:, :], ones_bf[0:1, :],
                                              bias_sb[U % 2][0:1, :],
                                              start=False, stop=True)
                        mmb.inc(inst)
                        mmb_rec[(g, ui, b)] = mmb.n

    chs_rec = {}
    chv_rec = {}
    chp_rec = {}
    qkr_rec = {}

    def uhg_scalar(eng, src_ap, sscol, par, scale=None, bias_ap=None):
        chs.inc(eng.activation(out=trash[par][:, :], in_=src_ap, func=AF.Square,
                               accum_out=stat[:, sscol:sscol + 1]))
        eng.wait_ge(chs.h, chs.n)
        if scale is None:
            chs.inc(eng.activation(out=stat[:, sscol + 16:sscol + 17],
                                   in_=stat[:, sscol:sscol + 1], func=AF.Sqrt))
        else:
            chs.inc(eng.activation(out=stat[:, sscol + 16:sscol + 17],
                                   in_=stat[:, sscol:sscol + 1], func=AF.Sqrt,
                                   scale=scale, bias=bias_ap))

    for g in range(NG):
        trc_base = g * D_TRC
        cvc_base = g * D_CVC
        aev_base = g * D_AEV
        mmb_base = g * D_MMB

        @blk.scalar
        def _(eng, g=g, trc_base=trc_base, aev_base=aev_base, mmb_base=mmb_base):
            for b in range(4):
                for half in range(2):
                    k = trc_base + b * 2 + half
                    eng.wait_ge(trc.h, 4 * k + 4)
                    if g >= 1 and b == 0 and half == 0:
                        eng.wait_ge(mmb.h, mmb_rec[(g - 1, 5, 3)])
                    tre.inc(eng.copy(out=xfm[b][:, half * 512:(half + 1) * 512],
                                     in_=pxt[k % 2][:, :]))
            if g >= 1:
                eng.wait_ge(rel.h, 4 * g)
                eng.wait_ge(hsq.h, 36 * g)
            for ui in range(6):
                proj, og = US[ui][0], US[ui][1]
                base = {"q": 0, "k": 4096, "v": 8192}[proj]
                for b in range(4):
                    eng.wait_ge(mmb.h, mmb_rec[(g, ui, b)])
                    aev.inc(eng.copy(
                        out=big[:, base + b * 1024 + og * 512
                                :base + b * 1024 + og * 512 + 512],
                        in_=pm[b][:, :]))
            for pi, proj in enumerate(("q", "k", "v")):
                base = {"q": 0, "k": 4096, "v": 8192}[proj]
                for b in range(4):
                    cidx = pi * 4 + b
                    if g * 12 + cidx >= 2:
                        eng.wait_ge(qkr.h, g * 12 + cidx - 1)
                    uhg_scalar(eng, big[:, base + b * 1024:base + b * 1024 + 1024],
                               cidx, cidx % 2)
                    chs_rec[("qkv", g, pi, b)] = chs.n

        @blk.vector
        def _(eng, g=g, trc_base=trc_base, cvc_base=cvc_base):
            for b in range(4):
                eng.wait_ge(xl[b].h, 16 * (g + 1))
                if g == 0 and b < 2:
                    eng.wait_ge(prep_vr.h, PREP_NT)
                    eng.wait_ge(prep_tr.h, PREP_NSUB)
                if g >= 1 and b < 2:
                    eng.wait_ge(trc.h, 4 * ((g - 1) * D_TRC + 24))
                if b >= 2:
                    eng.wait_ge(trc.h, 4 * (g * D_TRC + (b - 1) * 2))
                cvc.inc(eng.tensor_copy(out=cst[b % 2][:, :], in_=xres[b][:, :]))
            for pi, proj in enumerate(("q", "k", "v")):
                base = {"q": 0, "k": 4096, "v": 8192}[proj]
                for b in range(4):
                    cidx = pi * 4 + b
                    sscol = cidx
                    eng.wait_ge(chs.h, chs_rec[("qkv", g, pi, b)])
                    chv.inc(eng.tensor_scalar(
                        out=stat[:, sscol + 32:sscol + 33],
                        in0=stat[:, sscol + 16:sscol + 17],
                        scalar1=eps8[:, 0:1], scalar2=None, op0=ALU.add))
                    eng.wait_ge(chv.h, chv.n)
                    chv.inc(eng.reciprocal(out=stat[:, sscol + 48:sscol + 49],
                                           in_=stat[:, sscol + 32:sscol + 33]))
                    eng.wait_ge(chv.h, chv.n)
                    chv.inc(eng.tensor_scalar_mul(
                        out=big[:, base + b * 1024:base + b * 1024 + 1024],
                        in0=big[:, base + b * 1024:base + b * 1024 + 1024],
                        scalar1=stat[:, sscol + 48:sscol + 49]))
                    dst = (qn_sb if proj == "q" else kn_sb)
                    if proj != "v":
                        chv.inc(eng.tensor_reduce(
                            out=dst[:, b * 16:(b + 1) * 16],
                            in_=trash[cidx % 2][:, :].rearrange(
                                "p (h d) -> p h d", h=16),
                            axis=AX.X, op=ALU.add))
                        eng.wait_ge(chv.h, chv.n)
                        qkr.inc(eng.tensor_scalar(
                            out=dst[:, b * 16:(b + 1) * 16],
                            in0=dst[:, b * 16:(b + 1) * 16],
                            scalar1=stat[:, sscol + 48:sscol + 49],
                            scalar2=stat[:, sscol + 48:sscol + 49],
                            op0=ALU.mult, op1=ALU.mult))
                    else:
                        qkr.inc(eng.tensor_reduce(
                            out=ssum[:, b * 16:(b + 1) * 16],
                            in_=trash[cidx % 2][:, :].rearrange(
                                "p (h d) -> p h d", h=16),
                            axis=AX.X, op=ALU.add))
                    qkr_rec[g * 12 + cidx] = qkr.n

        @blk.vector
        def _(eng, g=g):
            eng.wait_ge(qkr.h, qkr_rec[g * 12 + 7])
            for pr in range(16):
                i, j = pr // 4, pr % 4
                chv.inc(eng.tensor_mul(out=tmpv[:, :],
                                       in0=big[:, qoff(i):qoff(i) + 1024],
                                       in1=big[:, koff(j):koff(j) + 1024]))
                eng.wait_ge(chv.h, chv.n)
                chv.inc(eng.tensor_reduce(
                    out=dots[:, pr * 16:(pr + 1) * 16],
                    in_=tmpv[:, :].rearrange("p (h d) -> p h d", h=16),
                    axis=AX.X, op=ALU.add))
                eng.wait_ge(chv.h, chv.n)
            for pr in range(16):
                i, j = pr // 4, pr % 4
                chv.inc(eng.tensor_mul(out=dens[:, pr * 16:(pr + 1) * 16],
                                       in0=qn_sb[:, i * 16:(i + 1) * 16],
                                       in1=kn_sb[:, j * 16:(j + 1) * 16]))
            eng.wait_ge(chv.h, chv.n)
            chv.inc(eng.tensor_scalar_add(out=dens[:, :], in0=dens[:, :],
                                          scalar1=1e-8))
            chv.inc(eng.reciprocal(out=dens[:, :], in_=dens[:, :]))
            eng.wait_ge(chv.h, chv.n)
            chv.inc(eng.tensor_mul(out=sqs[:, :], in0=dots[:, :], in1=dots[:, :]))
            eng.wait_ge(chv.h, chv.n)
            chv.inc(eng.tensor_tensor(out=sqs[:, :], in0=sqs[:, :],
                                      in1=dens[:, :], op=ALU.mult))
            eng.wait_ge(chv.h, chv.n)
            chv.inc(eng.tensor_scalar(out=sqs[:, :], in0=sqs[:, :],
                                      scalar1=-1.0, scalar2=1.0,
                                      op0=ALU.mult, op1=ALU.add))
            eng.wait_ge(chv.h, chv.n)
            chv.inc(eng.tensor_relu(out=sqs[:, :], in_=sqs[:, :]))
            chv_rec[("quad", g)] = chv.n

        @blk.scalar
        def _(eng, g=g):
            eng.wait_ge(chv.h, chv_rec[("quad", g)])
            chs.inc(eng.activation(out=sqs[:, :], in_=sqs[:, :], func=AF.Sqrt))
            chs_rec[("sqq", g)] = chs.n

        @blk.vector
        def _(eng, g=g):
            eng.wait_ge(chs.h, chs_rec[("sqq", g)])
            for i in range(4):
                chv.inc(eng.tensor_tensor(
                    out=smx[:, i * 16:(i + 1) * 16],
                    in0=sqs[:, (4 * i) * 16:(4 * i + 1) * 16],
                    in1=sqs[:, (4 * i + 1) * 16:(4 * i + 2) * 16], op=ALU.min))
                eng.wait_ge(chv.h, chv.n)
                chv.inc(eng.tensor_tensor(
                    out=smx[:, i * 16:(i + 1) * 16],
                    in0=smx[:, i * 16:(i + 1) * 16],
                    in1=sqs[:, (4 * i + 2) * 16:(4 * i + 3) * 16], op=ALU.min))
                eng.wait_ge(chv.h, chv.n)
                chv.inc(eng.tensor_tensor(
                    out=smx[:, i * 16:(i + 1) * 16],
                    in0=smx[:, i * 16:(i + 1) * 16],
                    in1=sqs[:, (4 * i + 3) * 16:(4 * i + 4) * 16], op=ALU.min))
                eng.wait_ge(chv.h, chv.n)
                for j in range(4):
                    pr = 4 * i + j
                    chv.inc(eng.tensor_sub(
                        out=eee[:, pr * 16:(pr + 1) * 16],
                        in0=smx[:, i * 16:(i + 1) * 16],
                        in1=sqs[:, pr * 16:(pr + 1) * 16]))
            chv_rec[("earg", g)] = chv.n

        @blk.scalar
        def _(eng, g=g):
            eng.wait_ge(chv.h, chv_rec[("earg", g)])
            chs.inc(eng.activation(out=eee[:, :], in_=eee[:, :], func=AF.Exp))
            chs_rec[("exp", g)] = chs.n

        @blk.vector
        def _(eng, g=g, cvc_base=cvc_base):
            eng.wait_ge(chs.h, chs_rec[("exp", g)])
            for i in range(4):
                chv.inc(eng.tensor_add(
                    out=ssum[:, i * 16:(i + 1) * 16],
                    in0=eee[:, (4 * i) * 16:(4 * i + 1) * 16],
                    in1=eee[:, (4 * i + 1) * 16:(4 * i + 2) * 16]))
                eng.wait_ge(chv.h, chv.n)
                chv.inc(eng.tensor_add(
                    out=ssum[:, i * 16:(i + 1) * 16],
                    in0=ssum[:, i * 16:(i + 1) * 16],
                    in1=eee[:, (4 * i + 2) * 16:(4 * i + 3) * 16]))
                eng.wait_ge(chv.h, chv.n)
                chv.inc(eng.tensor_add(
                    out=ssum[:, i * 16:(i + 1) * 16],
                    in0=ssum[:, i * 16:(i + 1) * 16],
                    in1=eee[:, (4 * i + 3) * 16:(4 * i + 4) * 16]))
                eng.wait_ge(chv.h, chv.n)
                chv.inc(eng.reciprocal(out=ssum[:, i * 16:(i + 1) * 16],
                                       in_=ssum[:, i * 16:(i + 1) * 16]))
                eng.wait_ge(chv.h, chv.n)
                for j in range(4):
                    pr = 4 * i + j
                    chv.inc(eng.tensor_tensor(
                        out=eee[:, pr * 16:(pr + 1) * 16],
                        in0=eee[:, pr * 16:(pr + 1) * 16],
                        in1=ssum[:, i * 16:(i + 1) * 16], op=ALU.mult))
            chv_rec[("attn", g)] = chv.n
            for b2_ in range(4):
                eng.wait_ge(chp.h, g * D_CHP + 7 * (b2_ + 1))
                eng.wait_ge(trc.h, 4 * (trc_base + 8) if b2_ < 2
                            else 4 * (trc_base + 8 + (b2_ - 1) * 2))
                cvc.inc(eng.tensor_copy(out=cst[b2_ % 2][:, :],
                                        in_=big[:, ooff(b2_):ooff(b2_) + 1024]))
            chv_rec[("ocast", g)] = chv.n

        @blk.gpsimd
        def _(eng, g=g):
            eng.wait_ge(chv.h, chv_rec[("attn", g)])
            eng.wait_ge(qkr.h, qkr_rec[g * 12 + 11])
            if g >= 1:
                eng.wait_ge(rel.h, 4 * g)
            for i in range(4):
                for j in range(4):
                    pr = 4 * i + j
                    a3 = eee[:, pr * 16:(pr + 1) * 16].unsqueeze(2) \
                        .broadcast_to([128, 16, 64])
                    if j == 0:
                        chp.inc(eng.tensor_tensor(
                            out=big[:, ooff(i):ooff(i) + 1024].rearrange(
                                "p (h d) -> p h d", h=16),
                            in0=big[:, voff(j):voff(j) + 1024].rearrange(
                                "p (h d) -> p h d", h=16),
                            in1=a3, op=ALU.mult))
                    else:
                        chp.inc(eng.tensor_tensor(
                            out=tmpp[:, :].rearrange("p (h d) -> p h d", h=16),
                            in0=big[:, voff(j):voff(j) + 1024].rearrange(
                                "p (h d) -> p h d", h=16),
                            in1=a3, op=ALU.mult))
                        eng.wait_ge(chp.h, chp.n)
                        chp.inc(eng.tensor_add(
                            out=big[:, ooff(i):ooff(i) + 1024],
                            in0=big[:, ooff(i):ooff(i) + 1024],
                            in1=tmpp[:, :]))
                    eng.wait_ge(chp.h, chp.n)

        @blk.scalar
        def _(eng, g=g, trc_base=trc_base, aev_base=aev_base):
            for b in range(4):
                for half in range(2):
                    k = trc_base + 8 + b * 2 + half
                    eng.wait_ge(trc.h, 4 * k + 4)
                    if g >= 1 and b == 0 and half == 0:
                        eng.wait_ge(mmb.h, mmb_rec[(g - 1, 7, 3)])
                    tre.inc(eng.copy(out=ofm[b][:, half * 512:(half + 1) * 512],
                                     in_=pxt[k % 2][:, :]))
            for ui in (6, 7):
                og = US[ui][1]
                for b in range(4):
                    eng.wait_ge(mmb.h, mmb_rec[(g, ui, b)])
                    if og == 0 and g >= 1:
                        eng.wait_ge(chv.h, chv_rec[("x2c", g - 1, b)])
                    aev.inc(eng.copy(
                        out=x1b[b][:, og * 512:og * 512 + 512], in_=pm[b][:, :]))
            eng.wait_ge(qkr.h, qkr_rec[g * 12 + 11])
            for b in range(4):
                cidx = 12 + b
                eng.wait_ge(aev.h, aev_base + 28 + b + 1)
                uhg_scalar(eng, x1b[b][:, :], 12 + (b % 4), cidx % 2)
                chs_rec[("ao", g, b)] = chs.n

        @blk.vector
        def _(eng, g=g):
            for b in range(4):
                sscol = 12 + (b % 4)
                eng.wait_ge(chs.h, chs_rec[("ao", g, b)])
                chv.inc(eng.tensor_scalar(
                    out=stat[:, sscol + 32:sscol + 33],
                    in0=stat[:, sscol + 16:sscol + 17],
                    scalar1=eps8[:, 0:1], scalar2=None, op0=ALU.add))
                eng.wait_ge(chv.h, chv.n)
                chv.inc(eng.reciprocal(out=stat[:, sscol + 48:sscol + 49],
                                       in_=stat[:, sscol + 32:sscol + 33]))
                eng.wait_ge(chv.h, chv.n)
                chv.inc(eng.scalar_tensor_tensor(
                    out=x1b[b][:, :], in0=x1b[b][:, :],
                    scalar=stat[:, sscol + 48:sscol + 49], in1=xres[b][:, :],
                    op0=ALU.mult, op1=ALU.add))
                eng.wait_ge(chv.h, chv.n)
                chv.inc(eng.reduce_sum(out=stat[:, sscol:sscol + 1],
                                       in_=x1b[b][:, :], axis=AX.X))
                eng.wait_ge(chv.h, chv.n)
                chv.inc(eng.tensor_scalar_mul(out=stat[:, sscol + 32:sscol + 33],
                                              in0=stat[:, sscol:sscol + 1],
                                              scalar1=1.0 / 1024.0))
                eng.wait_ge(chv.h, chv.n)
                chv.inc(eng.tensor_scalar_sub(out=x1b[b][:, :], in0=x1b[b][:, :],
                                              scalar1=stat[:, sscol + 32
                                                           :sscol + 33]))
                chv_rec[("aoc", g, b)] = chv.n

        @blk.scalar
        def _(eng, g=g):
            for b in range(4):
                sscol = 12 + (b % 4)
                cidx = 16 + b
                eng.wait_ge(chv.h, chv_rec[("aoc", g, b)])
                chs.inc(eng.activation(out=trash[cidx % 2][:, :], in_=x1b[b][:, :],
                                       func=AF.Square,
                                       accum_out=stat[:, sscol:sscol + 1]))
                eng.wait_ge(chs.h, chs.n)
                chs.inc(eng.activation(out=stat[:, sscol + 16:sscol + 17],
                                       in_=stat[:, sscol:sscol + 1],
                                       func=AF.Sqrt, scale=1.0 / 1023.0,
                                       bias=epsl[:, 0:1]))
                chs_rec[("ln1", g, b)] = chs.n

        @blk.vector
        def _(eng, g=g, cvc_base=cvc_base, trc_base=trc_base):
            for b in range(4):
                sscol = 12 + (b % 4)
                eng.wait_ge(chs.h, chs_rec[("ln1", g, b)])
                chv.inc(eng.reciprocal(out=stat[:, sscol + 48:sscol + 49],
                                       in_=stat[:, sscol + 16:sscol + 17]))
                eng.wait_ge(chv.h, chv.n)
                chv.inc(eng.tensor_scalar_mul(out=x1b[b][:, :], in0=x1b[b][:, :],
                                              scalar1=stat[:, sscol + 48
                                                           :sscol + 49]))
                eng.wait_ge(chv.h, chv.n)
                chv.inc(eng.scalar_tensor_tensor(
                    out=x1b[b][:, :], in0=x1b[b][:, :], scalar=1.0,
                    in1=gbt[:, 0:1024], op0=ALU.mult, op1=ALU.mult))
                eng.wait_ge(chv.h, chv.n)
                chv.inc(eng.tensor_add(out=x1b[b][:, :], in0=x1b[b][:, :],
                                       in1=gbt[:, 1024:2048]))
                eng.wait_ge(chv.h, chv.n)
                eng.wait_ge(trc.h, 4 * (g * D_TRC + 16) if b < 2
                            else 4 * (g * D_TRC + 16 + (b - 1) * 2))
                cvc.inc(eng.tensor_copy(out=cst[b % 2][:, :], in_=x1b[b][:, :]))

        @blk.scalar
        def _(eng, g=g, trc_base=trc_base, aev_base=aev_base):
            for b in range(4):
                for half in range(2):
                    k = trc_base + 16 + b * 2 + half
                    eng.wait_ge(trc.h, 4 * k + 4)
                    if g >= 1 and b == 0 and half == 0:
                        eng.wait_ge(mmb.h, mmb_rec[(g - 1, 15, 3)])
                    tre.inc(eng.copy(out=x1fm[b][:, half * 512:(half + 1) * 512],
                                     in_=pxt[k % 2][:, :]))
            for ui in range(8, 16):
                og = US[ui][1]
                for b in range(4):
                    eng.wait_ge(mmb.h, mmb_rec[(g, ui, b)])
                    aev.inc(eng.copy(
                        out=big[:, b * 4096 + og * 512:b * 4096 + og * 512 + 512],
                        in_=pm[b][:, :]))

        @blk.vector
        def _(eng, g=g, aev_base=aev_base):
            for b in range(4):
                for og in range(8):
                    eng.wait_ge(aev.h, aev_base + 32 + og * 4 + b + 1)
                    eng.wait_ge(hsq.h, hsq.n)
                    hsq.inc(eng.scalar_tensor_tensor(
                        out=trash[0][:, 0:512],
                        in0=big[:, b * 4096 + og * 512:b * 4096 + og * 512 + 512],
                        scalar=1.0,
                        in1=big[:, b * 4096 + og * 512:b * 4096 + og * 512 + 512],
                        op0=ALU.mult, op1=ALU.mult,
                        accum_out=stat[:, 32 + b * 8 + og:33 + b * 8 + og]))
                eng.wait_ge(hsq.h, hsq.n)
                hsq.inc(eng.tensor_reduce(
                    out=stat[:, 12 + b:13 + b],
                    in_=stat[:, 32 + b * 8:32 + b * 8 + 8].unsqueeze(1),
                    axis=AX.X, op=ALU.add))
                chv_rec[("hss", g, b)] = hsq.n

        @blk.scalar
        def _(eng, g=g, trc_base=trc_base):
            for b in range(4):
                eng.wait_ge(hsq.h, chv_rec[("hss", g, b)])
                chs.inc(eng.activation(out=stat[:, 28 + b:29 + b],
                                       in_=stat[:, 12 + b:13 + b], func=AF.Sqrt))
                chs_rec[("hn", g, b)] = chs.n

        @blk.vector
        def _(eng, g=g):
            for b in range(4):
                eng.wait_ge(chs.h, chs_rec[("hn", g, b)])
                chv.inc(eng.tensor_scalar(
                    out=stat[:, 44 + b:45 + b], in0=stat[:, 28 + b:29 + b],
                    scalar1=eps8[:, 0:1], scalar2=None, op0=ALU.add))
                eng.wait_ge(chv.h, chv.n)
                chv.inc(eng.reciprocal(out=stat[:, 60 + b:61 + b],
                                       in_=stat[:, 44 + b:45 + b]))
                chv_rec[("hrec", g, b)] = chv.n

        @blk.scalar
        def _(eng, g=g, trc_base=trc_base, aev_base=aev_base):
            for bp in range(2):
                for b in (2 * bp, 2 * bp + 1):
                    eng.wait_ge(chv.h, chv_rec[("hrec", g, b)])
                    if g * 4 + b >= 1:
                        if b == 0:
                            eng.wait_ge(trc.h, 4 * trc_base)
                        else:
                            eng.wait_ge(trc.h, 4 * (trc_base + 24
                                        + (16 if b >= 2 else 0)
                                        + (8 if b % 2 == 1 else 0)))
                    rel.inc(eng.activation(out=hbf[0][:, :],
                                           in_=big[:, b * 4096:(b + 1) * 4096],
                                           func=AF.Relu,
                                           scale=stat[:, 60 + b:61 + b]))
                    for half in range(8):
                        k = trc_base + 24 + bp * 16 + (b - 2 * bp) * 8 + half
                        eng.wait_ge(trc.h, 4 * k + 4)
                        if g >= 1 and b == 0 and half == 0:
                            eng.wait_ge(mmb.h, mmb_rec[(g - 1, 31, 3)])
                        tre.inc(eng.copy(
                            out=hfm[b % 2][:, half * 512:(half + 1) * 512],
                            in_=pxt[k % 2][:, :]))
                for uii in (16 + bp * 8 + 3, 16 + bp * 8 + 7):
                    og = US[uii][1]
                    for b in (2 * bp, 2 * bp + 1):
                        eng.wait_ge(mmb.h, mmb_rec[(g, uii, b)])
                        if g >= 1 and og == 0:
                            pidx = (g - 1) * 4 + b
                            eng.wait_ge(yst[pidx % 2].h, 16 * (pidx // 2 + 1))
                        aev.inc(eng.copy(
                            out=ffb[b][:, og * 512:og * 512 + 512],
                            in_=pm[b - 2 * bp][:, :]))
            eng.wait_ge(qkr.h, qkr_rec[g * 12 + 11])
            for b in range(4):
                cidx = 20 + b
                uhg_scalar(eng, ffb[b][:, :], 12 + (b % 4), cidx % 2)
                chs_rec[("ff", g, b)] = chs.n

        @blk.vector
        def _(eng, g=g):
            for b in range(4):
                idx = g * 4 + b
                sscol = 12 + (b % 4)
                eng.wait_ge(chs.h, chs_rec[("ff", g, b)])
                chv.inc(eng.tensor_scalar(
                    out=stat[:, sscol + 32:sscol + 33],
                    in0=stat[:, sscol + 16:sscol + 17],
                    scalar1=eps8[:, 0:1], scalar2=None, op0=ALU.add))
                eng.wait_ge(chv.h, chv.n)
                chv.inc(eng.reciprocal(out=stat[:, sscol + 48:sscol + 49],
                                       in_=stat[:, sscol + 32:sscol + 33]))
                eng.wait_ge(chv.h, chv.n)
                chv.inc(eng.scalar_tensor_tensor(
                    out=ffb[b][:, :], in0=ffb[b][:, :],
                    scalar=stat[:, sscol + 48:sscol + 49], in1=x1b[b][:, :],
                    op0=ALU.mult, op1=ALU.add))
                eng.wait_ge(chv.h, chv.n)
                chv.inc(eng.reduce_sum(out=stat[:, sscol:sscol + 1],
                                       in_=ffb[b][:, :], axis=AX.X))
                eng.wait_ge(chv.h, chv.n)
                chv.inc(eng.tensor_scalar_mul(out=stat[:, sscol + 32:sscol + 33],
                                              in0=stat[:, sscol:sscol + 1],
                                              scalar1=1.0 / 1024.0))
                eng.wait_ge(chv.h, chv.n)
                chv.inc(eng.tensor_scalar_sub(
                    out=ffb[b][:, :], in0=ffb[b][:, :],
                    scalar1=stat[:, sscol + 32:sscol + 33]))
                chv_rec[("x2c", g, b)] = chv.n

        @blk.scalar
        def _(eng, g=g):
            for b in range(4):
                sscol = 12 + (b % 4)
                cidx = 24 + b
                eng.wait_ge(chv.h, chv_rec[("x2c", g, b)])
                chs.inc(eng.activation(out=trash[cidx % 2][:, :], in_=ffb[b][:, :],
                                       func=AF.Square,
                                       accum_out=stat[:, sscol:sscol + 1]))
                eng.wait_ge(chs.h, chs.n)
                chs.inc(eng.activation(out=stat[:, sscol + 16:sscol + 17],
                                       in_=stat[:, sscol:sscol + 1],
                                       func=AF.Sqrt, scale=1.0 / 1023.0,
                                       bias=epsl[:, 0:1]))
                chs_rec[("ln2", g, b)] = chs.n

        @blk.vector
        def _(eng, g=g):
            for b in range(4):
                sscol = 12 + (b % 4)
                eng.wait_ge(chs.h, chs_rec[("ln2", g, b)])
                chv.inc(eng.reciprocal(out=stat[:, sscol + 48:sscol + 49],
                                       in_=stat[:, sscol + 16:sscol + 17]))
                eng.wait_ge(chv.h, chv.n)
                chv.inc(eng.tensor_scalar_mul(out=ffb[b][:, :], in0=ffb[b][:, :],
                                              scalar1=stat[:, sscol + 48
                                                           :sscol + 49]))
                eng.wait_ge(chv.h, chv.n)
                chv.inc(eng.scalar_tensor_tensor(
                    out=ffb[b][:, :], in0=ffb[b][:, :], scalar=1.0,
                    in1=gbt[:, 2048:3072], op0=ALU.mult, op1=ALU.mult))
                eng.wait_ge(chv.h, chv.n)
                chv.inc(eng.tensor_add(out=ffb[b][:, :], in0=ffb[b][:, :],
                                       in1=gbt[:, 3072:4096]))
                chv_rec[("x2", g, b)] = chv.n

        @blk.scalar
        def _(eng, g=g):
            for b in range(4):
                sscol = 12 + (b % 4)
                cidx = 28 + b
                eng.wait_ge(chv.h, chv_rec[("x2", g, b)])
                chs.inc(eng.activation(out=trash[cidx % 2][:, :], in_=ffb[b][:, :],
                                       func=AF.Square,
                                       accum_out=stat[:, sscol:sscol + 1]))
                eng.wait_ge(chs.h, chs.n)
                chs.inc(eng.activation(out=stat[:, sscol + 16:sscol + 17],
                                       in_=stat[:, sscol:sscol + 1],
                                       func=AF.Sqrt))
                chs_rec[("slot", g, b)] = chs.n

        @blk.vector
        def _(eng, g=g):
            for b in range(4):
                idx = g * 4 + b
                sscol = 12 + (b % 4)
                eng.wait_ge(chs.h, chs_rec[("slot", g, b)])
                chv.inc(eng.reciprocal(out=stat[:, sscol + 48:sscol + 49],
                                       in_=stat[:, sscol + 16:sscol + 17]))
                eng.wait_ge(chv.h, chv.n)
                chv.inc(eng.scalar_tensor_tensor(
                    out=ffb[b][:, :], in0=ffb[b][:, :],
                    scalar=stat[:, sscol + 48:sscol + 49], in1=xres[b][:, :],
                    op0=ALU.mult, op1=ALU.add))
                eng.wait_ge(chv.h, chv.n)
                chv.inc(eng.reduce_sum(out=stat[:, sscol:sscol + 1],
                                       in_=ffb[b][:, :], axis=AX.X))
                eng.wait_ge(chv.h, chv.n)
                chv.inc(eng.tensor_scalar_mul(out=stat[:, sscol + 32:sscol + 33],
                                              in0=stat[:, sscol:sscol + 1],
                                              scalar1=1.0 / 1024.0))
                eng.wait_ge(chv.h, chv.n)
                chv.inc(eng.tensor_scalar_sub(
                    out=ffb[b][:, :], in0=ffb[b][:, :],
                    scalar1=stat[:, sscol + 32:sscol + 33]))
                chv_rec[("yc", g, b)] = chv.n

        @blk.scalar
        def _(eng, g=g):
            for b in range(4):
                idx = g * 4 + b
                sscol = 12 + (b % 4)
                cidx = 32 + b
                eng.wait_ge(chv.h, chv_rec[("yc", g, b)])
                chs.inc(eng.activation(out=trash[cidx % 2][:, :],
                                       in_=ffb[b][:, :], func=AF.Square,
                                       accum_out=stat[:, sscol:sscol + 1]))
                eng.wait_ge(chs.h, chs.n)
                chs.inc(eng.activation(out=stat[:, sscol + 16:sscol + 17],
                                       in_=stat[:, sscol:sscol + 1],
                                       func=AF.Sqrt, scale=1.0 / 1024.0,
                                       bias=epsl[:, 0:1]))
                chs_rec[("fln", g, b)] = chs.n

        @blk.vector
        def _(eng, g=g):
            for b in range(4):
                idx = g * 4 + b
                sscol = 12 + (b % 4)
                eng.wait_ge(chs.h, chs_rec[("fln", g, b)])
                chv.inc(eng.reciprocal(out=stat[:, sscol + 48:sscol + 49],
                                       in_=stat[:, sscol + 16:sscol + 17]))
                eng.wait_ge(chv.h, chv.n)
                epi.inc(eng.tensor_scalar_mul(
                    out=ffb[b][:, :], in0=ffb[b][:, :],
                    scalar1=stat[:, sscol + 48:sscol + 49]))

        @blk.gpsimd
        def _(eng, g=g):
            for b in range(4):
                idx = g * 4 + b
                gi = b * NG + g
                eng.wait_ge(epi.h, idx + 1)
                yst[idx % 2].inc(eng.dma_start(
                    out=y_out[gi * 128:(gi + 1) * 128, :],
                    in_=ffb[b][:, :]), 16)

    NTOT = NG * 4

    @blk.gpsimd
    def _(eng):
        eng.wait_ge(yst[0].h, 16 * ((NTOT + 1) // 2))
        eng.wait_ge(yst[1].h, 16 * (NTOT // 2))

    p.close()
    return p


def _l1a_in_maps(inputs, x_full):
    idn = ident_np()
    NT = POS * B
    maps = []
    for c in range(NCORES):
        m = {"x": np.ascontiguousarray(x_full[c * NT:(c + 1) * NT]),
             "ident": idn}
        for k in ("wq", "bq", "wk", "bk", "wv", "bv", "wo", "bo", "w1", "b1",
                  "w2", "b2", "g1", "beta1", "g2", "beta2"):
            m[k] = inputs["L1_attn_" + k]
        maps.append(m)
    return maps


def run_l1a(inputs, x_bsd):
    p = build_l1a()
    xc = tokens_of_core(x_bsd)
    maps = _l1a_in_maps(inputs, xc.reshape(NCORES * B * POS, D))
    res = run_bass_kernel_spmd(p.nc, maps, core_ids=list(range(NCORES)))
    y = np.stack([res.results[c]["y"] for c in range(NCORES)])
    return tokens_to_bsd(y)



INV2PI = float(1.0 / (2.0 * np.pi))
RCONST = 12582912.0
TWO_PI = float(2.0 * np.pi)


def build_l0a(NPOS=POS, SFULL=S):
    NQT = NPOS // 128
    NKQ = 8
    KQT = SFULL // NKQ // 128
    QW = min(512, NPOS)
    NQG = NPOS // QW
    QTPG = QW // 128
    p = Prog()
    x_in = p.inp("x", [B, SFULL, D])
    xq_in = p.inp("xq", [B, NPOS, D])
    th_in = p.inp("theta", [16, 64])
    wc_in = p.inp("W_c", [1024, 1024])
    bc_in = p.inp("b_c", [1024])
    idn = p.inp("ident", [128, 128], BF16)
    y_out = p.out("y", [B * NPOS, D])

    wcf = p.dram("wcf", [2, 128, 8, 512], BF16)

    ident_sb = p.sb("identb", [128, 128], BF16)
    identf = p.sb("identf", [128, 128], F32)
    qfm = p.sb("qfm", [128, 8 * NPOS], BF16)
    kfm = p.sb("kfm", [128, 8 * 1024], BF16)
    ptm = p.sb("ptm", [128, 8 * 1024], BF16)
    esb = p.sb("esb", [128, 8 * 512], BF16)
    oacc = p.sb("oacc", [128, NQT * 1024], F32)
    sacc = p.sb("sacc", [128, 16], F32)
    wcs = p.sb("wcs", [128, 2 * 4096], BF16)
    ub = [p.sb(f"ubL{i}", [128, 1024], F32) for i in range(2)]
    pb = [p.sb(f"pbL{i}", [128, 1024], F32) for i in range(2)]
    sb_ = [p.sb(f"sinbL{i}", [128, 1024], F32) for i in range(2)]
    trash = [p.sb(f"trshL{i}", [128, 1024], F32) for i in range(2)]
    trashE = p.sb("trashE", [128, 1024], F32)
    norms = p.sb("normsL", [128, 8], F32)
    statL = p.sb("statL", [128, 16], F32)
    rowbuf = p.sb("rowbufL", [128, 128], F32)
    ctb = p.sb("ctbL", [128, 1024], F32)
    bcb = p.sb("bcbL", [128, 1024], F32)
    negC = p.sb("negCL", [128, 1], F32)
    epsl = p.sb("epslL", [128, 1], F32)
    ones_bf = p.sb("onesbL", [128, 1], BF16)
    onesr = p.sb("onesrL", [1, 128], F32)
    throw = p.sb("throwL", [1, 1024], F32)
    bcrow = p.sb("bcrowL", [1, 1024], F32)
    xrb = [p.sb(f"xrbL{i}", [128, 1024], F32) for i in range(2)]
    yb = [p.sb(f"ybL{i}", [128, 1024], F32) for i in range(2)]
    cstL = [p.sb(f"cstL{i}", [128, 1024], BF16) for i in range(2)]
    ofmL = p.sb("ofmL", [128, 8 * 128], BF16)
    wldL = [p.sb(f"wldL{i}", [128, 1024], F32) for i in range(2)]
    wbfL = [p.sb(f"wbfL{i}", [128, 1024], BF16) for i in range(2)]
    wtrL = [p.sb(f"wtrL{i}", [128, 512], BF16) for i in range(2)]
    scolL = [p.sb(f"scL{i}", [128, 1], F32) for i in range(2)]

    pmS = [p.ps(f"pmSL{i}", [128, 512], F32) for i in range(2)]
    pmA = [p.ps(f"pmAL{i}", [128, 512], F32) for i in range(2)]
    pmX = p.ps("pmXL", [128, 512], F32)
    pxt = [p.ps(f"pxtL{i}", [128, 512], BF16) for i in range(2)]

    blk = p.block()

    ild = p.sem("ild")
    cns = p.sem("cns")
    chv = p.sem("chv")
    chp = p.sem("chp")
    chs = p.sem("chs")
    cht = p.sem("cht")
    xl = [p.sem(f"xlL{i}") for i in range(2)]
    xrl = [p.sem(f"xrlL{i}") for i in range(2)]
    trc = p.sem("trc")
    tre = p.sem("tre")
    cvD = p.sem("cvD")
    sqD = p.sem("sqD")
    snD = p.sem("snD")
    aevK = p.sem("aevK")
    mmS = p.sem("mmS")
    mmA = p.sem("mmA")
    avE = p.sem("avE")
    wcl = p.sem("wcl")
    epi = p.sem("epi")
    yst = [p.sem(f"ystL{i}") for i in range(2)]

    @blk.sync
    def _(eng):
        ild.inc(eng.dma_start(out=ident_sb[:, :], in_=idn), 16)
        ild.inc(eng.dma_start(out=ub[0][0:1, :],
                              in_=th_in.rearrange("a b -> (a b)").unsqueeze(0)),
                16)
        ild.inc(eng.dma_start(out=pb[0][0:1, :], in_=bc_in.unsqueeze(0)), 16)
    ILD0 = ild.n

    @blk.vector
    def _(eng):
        eng.wait_ge(ild.h, ILD0)
        cns.inc(eng.memset(epsl[:, :], 1e-5))
        cns.inc(eng.memset(ones_bf[:, :], 1.0))
        cns.inc(eng.memset(onesr[:, :], 1.0))
        cns.inc(eng.tensor_copy(out=identf[:, :], in_=ident_sb[:, :]))
        cns.inc(eng.tensor_scalar(out=throw[0:1, :], in0=ub[0][0:1, :],
                                  scalar1=INV2PI, scalar2=0.25,
                                  op0=ALU.mult, op1=ALU.add))
        eng.wait_ge(cns.h, cns.n)
        cns.inc(eng.tensor_scalar_add(out=bcrow[0:1, :], in0=throw[0:1, :],
                                      scalar1=RCONST))
        eng.wait_ge(cns.h, cns.n)
        cns.inc(eng.scalar_tensor_tensor(out=bcrow[0:1, :], in0=bcrow[0:1, :],
                                         scalar=-RCONST, in1=throw[0:1, :],
                                         op0=ALU.add, op1=ALU.subtract))
    CNS1 = cns.n

    @blk.scalar
    def _(eng):
        eng.wait_ge(cns.h, CNS1)
        chs.inc(eng.activation(out=throw[0:1, :], in_=bcrow[0:1, :],
                               func=AF.Sin, scale=-TWO_PI))
    CHS1 = chs.n

    @blk.tensor
    def _(eng):
        eng.wait_ge(chs.h, CHS1)
        for i in range(4):
            if i >= 1:
                eng.wait_ge(chs.h, CHS1 + i)
            src = throw if i < 2 else pb[0][0:1, :]
            half = i % 2
            cht.inc(eng.matmul(pmX[:, :], onesr[0:1, :],
                               (throw[0:1, half * 512:(half + 1) * 512]
                                if i < 2 else
                                pb[0][0:1, half * 512:(half + 1) * 512]),
                               start=True, stop=True))
    CHT1 = cht.n

    @blk.scalar
    def _(eng):
        for i in range(4):
            dstt = ctb if i < 2 else bcb
            half = i % 2
            eng.wait_ge(cht.h, i + 1)
            chs.inc(eng.copy(out=dstt[:, half * 512:(half + 1) * 512],
                             in_=pmX[:, :]))
    CHS2 = chs.n

    prepL = emit_weight_prep(
        p, blk, ident_sb, pxt,
        [(wc_in, None, wcf, 1024, 1024, "grouped512")], "wc",
        wldL, wbfL, wtrL, scolL, igw_max=1024)
    stL0, stL1 = prepL["st"]
    ST_WC = prepL["job_tgt"][0]
    PREPL_NT, PREPL_NSUB = prepL["nt"], prepL["nsub"]
    prepL_vr, prepL_tr, prepL_ev = prepL["vr"], prepL["tr"], prepL["ev"]

    @blk.sync
    def _(eng):
        eng.wait_ge(stL0.h, ST_WC[0])
        eng.wait_ge(stL1.h, ST_WC[1])
        for og in range(2):
            wcl.inc(eng.dma_start(out=wcs[:, og * 4096:(og + 1) * 4096],
                                  in_=wcf[og, :, :, :]), 16)

    cnt = dict(tile=0, half=0, nsq=0)
    xl_rec, u_rec, p1_rec, d_rec, sin_rec, cast_rec, sq_rec = \
        {}, {}, {}, {}, {}, {}, {}
    mul_rec = {}
    half_done = {}
    half_ids = {}
    tile_fm_done = {}

    def emit_build(tiles, dst_fm, dst_tm, dst_fm_w, fmtag, src_of, do_norms,
                   evac_wait=None, cast_wait=None):
        t0 = cnt["tile"]
        n = len(tiles)
        sqD_base = cnt["nsq"]
        halfbase = cnt["half"]

        @blk.vector
        def _(eng):
            def emit_u(i):
                t = t0 + i
                eng.wait_ge(xl[t % 2].h, 16 * (t // 2 + 1))
                if t == 0:
                    eng.wait_ge(chs.h, CHS2)
                if t >= 2:
                    eng.wait_ge(chp.h, t - 1)
                    eng.wait_ge(chv.h, d_rec[t - 2])
                chv.inc(eng.tensor_scalar(out=ub[t % 2][:, :],
                                          in0=ub[t % 2][:, :],
                                          scalar1=INV2PI, scalar2=0.25,
                                          op0=ALU.mult, op1=ALU.add))
                u_rec[t] = chv.n

            def emit_negd(i):
                t = t0 + i
                eng.wait_ge(chp.h, t + 1)
                if t >= 2:
                    eng.wait_ge(cvD.h, t - 1)
                    if do_norms and i >= 2:
                        eng.wait_ge(sqD.h, sqD_base + i - 1)
                chv.inc(eng.scalar_tensor_tensor(
                    out=sb_[t % 2][:, :], in0=pb[t % 2][:, :], scalar=-RCONST,
                    in1=ub[t % 2][:, :], op0=ALU.add, op1=ALU.subtract))
                d_rec[t] = chv.n

            def emit_mulcast(i):
                t = t0 + i
                eng.wait_ge(snD.h, t + 1)
                chv.inc(eng.tensor_mul(out=sb_[t % 2][:, :], in0=sb_[t % 2][:, :],
                                       in1=ctb[:, :]))
                mul_rec[t] = chv.n
                eng.wait_ge(chv.h, chv.n)
                dst = (dst_tm[:, tiles[i] * 1024:(tiles[i] + 1) * 1024]
                       if dst_tm is not None else cstL[t % 2][:, :])
                if i == 0 and cast_wait is not None:
                    eng.wait_ge(cast_wait[0].h, cast_wait[1])
                if dst_tm is None:
                    eng.wait_ge(trc.h, 4 * (halfbase + 2 * (i - 2) + 2)
                                if i >= 2 else 4 * halfbase)
                cvD.inc(eng.tensor_copy(out=dst, in_=sb_[t % 2][:, :]))

            for i in range(n + 2):
                if i < n:
                    emit_u(i)
                if 1 <= i <= n:
                    emit_negd(i - 1)
                if i >= 2:
                    emit_mulcast(i - 2)

        @blk.gpsimd
        def _(eng):
            for i, ti in enumerate(tiles):
                t = t0 + i
                eng.wait_ge(chv.h, u_rec[t])
                if t >= 2:
                    eng.wait_ge(chv.h, d_rec[t - 2])
                chp.inc(eng.tensor_scalar_add(out=pb[t % 2][:, :],
                                              in0=ub[t % 2][:, :],
                                              scalar1=RCONST))

        @blk.scalar
        def _(eng):
            first = [True]
            for i in range(n + 2):
                if do_norms and i >= 2:
                    t = t0 + i - 2
                    ti2 = tiles[i - 2]
                    eng.wait_ge(chv.h, mul_rec[t])
                    eng.wait_ge(cvD.h, t + 1)
                    eng.wait_ge(sqD.h, sqD.n)
                    sqD.inc(eng.activation(out=trash[t % 2][:, :],
                                           in_=sb_[t % 2][:, :], func=AF.Square,
                                           accum_out=norms[:, ti2:ti2 + 1]))
                if i >= 2:
                    t = t0 + i - 2
                    for hh in range(2):
                        h = halfbase + 2 * (i - 2) + hh
                        eng.wait_ge(trc.h, 4 * (h + 1))
                        if first[0] and evac_wait is not None:
                            eng.wait_ge(evac_wait[0].h, evac_wait[1])
                            first[0] = False
                        tii = tiles[i - 2]
                        dst = dst_fm[:, :].rearrange(
                            "p (c w) -> p c w", w=dst_fm_w)[
                            :, hh * 4:hh * 4 + 4, tii * 128:(tii + 1) * 128]
                        tre.inc(eng.copy(out=dst, in_=pxt[h % 2][:, :]))
                        tile_fm_done[(fmtag, tii)] = tre.n
                if i < n:
                    t = t0 + i
                    eng.wait_ge(chv.h, d_rec[t])
                    snD.inc(eng.activation(out=sb_[t % 2][:, :],
                                           in_=sb_[t % 2][:, :],
                                           func=AF.Sin, scale=-TWO_PI))
            if do_norms:
                cnt["nsq"] = sqD.n

        @blk.tensor
        def _(eng):
            for i, ti in enumerate(tiles):
                t = t0 + i
                eng.wait_ge(cvD.h, t + 1)
                srcb = (dst_tm[:, ti * 1024:(ti + 1) * 1024]
                        if dst_tm is not None else cstL[t % 2][:, :])
                for half in range(2):
                    h = cnt["half"]
                    cnt["half"] += 1
                    if h >= 2:
                        eng.wait_ge(tre.h, h - 1)
                    for bi in range(4):
                        trc.inc(eng.transpose(
                            pxt[h % 2][:, bi * 128:(bi + 1) * 128],
                            srcb[:, half * 512 + bi * 128
                                 :half * 512 + (bi + 1) * 128],
                            ident_sb[:, :]))

        @blk.sync
        def _(eng):
            for i, ti in enumerate(tiles):
                t = t0 + i
                if t >= 2:
                    eng.wait_ge(chv.h, d_rec[t - 2])
                    eng.wait_ge(chp.h, t - 1)
                xl[t % 2].inc(eng.dma_start(out=ub[t % 2][:, :], in_=src_of(ti)),
                              16)
        cnt["tile"] = t0 + n

    mk_rec = {}
    negC_rec = {}
    xrl_rec, ocast_rec, ofm_tr, ofm_ev = {}, {}, {}, {}
    wc_mm_rec, wc_ev_rec, ln_c_rec, ln_s_rec = {}, {}, {}, {}
    aev_pred = 0
    av_pred = 0
    mmA_rec = {}
    mmS_rec = {}
    esb_free_mmA = 0
    NT_Y = B * NQT
    ystc = [0, 0]

    for b in range(B):
        emit_build(list(range(NQT)), qfm, None, NPOS, ("q", b),
                   (lambda ti, b=b: xq_in[b, ti * 128:(ti + 1) * 128, :]),
                   do_norms=True,
                   evac_wait=(mmS, mmS.n) if b > 0 else None)

        @blk.vector
        def _(eng, b=b, _sq=None):
            eng.wait_ge(sqD.h, cnt["nsq"])
            chv.inc(eng.reduce_max(out=statL[:, 0:1], in_=norms[:, 0:NQT],
                                   axis=AX.X))
            mk_rec[b] = chv.n

        @blk.tensor
        def _(eng, b=b):
            eng.wait_ge(chv.h, mk_rec[b])
            cht.inc(eng.transpose(pmX[0:1, 0:128], statL[:, 0:1], identf[:, :]))
        cht_t1 = cht.n

        @blk.scalar
        def _(eng, b=b, cht_t1=cht_t1):
            eng.wait_ge(cht.h, cht_t1)
            chs.inc(eng.copy(out=rowbuf[0:1, 0:128], in_=pmX[0:1, 0:128]))
        chs_t1 = chs.n

        @blk.vector
        def _(eng, b=b, chs_t1=chs_t1):
            eng.wait_ge(chs.h, chs_t1)
            chv.inc(eng.reduce_max(out=statL[0:1, 1:2], in_=rowbuf[0:1, 0:128],
                                   axis=AX.X))
        chv_t1 = chv.n

        @blk.scalar
        def _(eng, b=b, chv_t1=chv_t1):
            eng.wait_ge(chv.h, chv_t1)
            chs.inc(eng.activation(out=statL[0:1, 2:3], in_=statL[0:1, 1:2],
                                   func=AF.Sqrt))
        chs_t2 = chs.n

        @blk.vector
        def _(eng, b=b, chs_t2=chs_t2):
            eng.wait_ge(chs.h, chs_t2)
            chv.inc(eng.tensor_scalar_mul(out=statL[0:1, 3:4],
                                          in0=statL[0:1, 2:3], scalar1=-4.0))
        chv_t2 = chv.n

        @blk.tensor
        def _(eng, b=b, chv_t2=chv_t2):
            eng.wait_ge(chv.h, chv_t2)
            cht.inc(eng.matmul(pmX[:, 0:1], onesr[0:1, :], statL[0:1, 3:4],
                               start=True, stop=True))
        cht_t2 = cht.n

        @blk.scalar
        def _(eng, b=b, cht_t2=cht_t2):
            eng.wait_ge(cht.h, cht_t2)
            chs.inc(eng.copy(out=negC[:, :], in_=pmX[:, 0:1]))
            negC_rec[b] = chs.n

        for kq in range(NKQ):
            emit_build(list(range(KQT)), kfm, ptm, 1024, ("k", b, kq),
                       (lambda ti, b=b, kq=kq: x_in[
                           b, (kq * KQT + ti) * 128:(kq * KQT + ti + 1) * 128,
                           :]),
                       do_norms=False,
                       evac_wait=(mmS, mmS.n) if (b > 0 or kq > 0) else None,
                       cast_wait=(mmA, mmA.n) if (b > 0 or kq > 0) else None)

            for qg in range(NQG):
                @blk.tensor
                def _(eng, b=b, kq=kq, qg=qg):
                    for kc in range(KQT):
                        if kc == 0 and qg == 0 and kq == 0:
                            eng.wait_ge(tre.h,
                                        tile_fm_done[(("q", b), NQT - 1)])
                        if kc >= 2:
                            eng.wait_ge(aevK.h, aev_pred + kc - 1)
                        eng.wait_ge(tre.h, tile_fm_done[(("k", b, kq), kc)])
                        for dc in range(8):
                            inst = eng.matmul(
                                pmS[kc % 2][:, 0:QW],
                                kfm[:, :].rearrange("p (c w) -> p c w", w=1024)
                                [:, dc, kc * 128:(kc + 1) * 128],
                                qfm[:, :].rearrange("p (c w) -> p c w", w=NPOS)
                                [:, dc, qg * QW:(qg + 1) * QW],
                                start=(dc == 0), stop=(dc == 7))
                        mmS.inc(inst)
                        mmS_rec[(b, kq, qg, kc)] = mmS.n

                @blk.scalar
                def _(eng, b=b, kq=kq, qg=qg):
                    nonlocal_esb_free = esb_free_mmA
                    for kc in range(KQT):
                        eng.wait_ge(mmS.h, mmS_rec[(b, kq, qg, kc)])
                        if kc == 0:
                            eng.wait_ge(chs.h, negC_rec[b])
                            eng.wait_ge(mmA.h, nonlocal_esb_free)
                        aevK.inc(eng.activation(
                            out=esb[:, kc * QW:(kc + 1) * QW],
                            in_=pmS[kc % 2][:, 0:QW], func=AF.Exp,
                            scale=0.125, bias=negC[:, 0:1]))
                aev_pred += KQT

                @blk.tensor
                def _(eng, b=b, kq=kq, qg=qg, avp0=avE.n):
                    for qt in range(QTPG):
                        if avp0 + 3 * qt > 0:
                            eng.wait_ge(avE.h, avp0 + 3 * qt)
                        for kc in range(KQT):
                            if kc == 0:
                                eng.wait_ge(aevK.h, aev_pred)
                            lhs = esb[:, kc * QW + qt * 128
                                      :kc * QW + (qt + 1) * 128]
                            for hh in range(2):
                                inst = eng.matmul(
                                    pmA[hh][:, :], lhs,
                                    ptm[:, kc * 1024 + hh * 512
                                        :kc * 1024 + (hh + 1) * 512],
                                    start=(kc == 0), stop=(kc == KQT - 1))
                                if kc == KQT - 1:
                                    mmA.inc(inst)
                            inst = eng.matmul(pmX[:, 0:1], lhs, ones_bf[:, 0:1],
                                              start=(kc == 0),
                                              stop=(kc == KQT - 1))
                        mmA.inc(inst)
                        mmA_rec[(b, kq, qg, qt)] = mmA.n

                @blk.vector
                def _(eng, b=b, kq=kq, qg=qg):
                    eng.wait_ge(chv.h, chv.n)
                    for qt in range(QTPG):
                        eng.wait_ge(mmA.h, mmA_rec[(b, kq, qg, qt)])
                        qslot = qg * QTPG + qt
                        if kq == 0:
                            avE.inc(eng.tensor_copy(
                                out=oacc[:, qslot * 1024:qslot * 1024 + 512],
                                in_=pmA[0][:, :]))
                            eng.wait_ge(chv.h, chv.n)
                            avE.inc(eng.tensor_copy(
                                out=oacc[:, qslot * 1024 + 512
                                         :(qslot + 1) * 1024],
                                in_=pmA[1][:, :]))
                            eng.wait_ge(chv.h, chv.n)
                            avE.inc(eng.tensor_copy(
                                out=sacc[:, qslot:qslot + 1], in_=pmX[:, 0:1]))
                            eng.wait_ge(chv.h, chv.n)
                        else:
                            avE.inc(eng.scalar_tensor_tensor(
                                out=oacc[:, qslot * 1024:qslot * 1024 + 512],
                                in0=pmA[0][:, :], scalar=1.0,
                                in1=oacc[:, qslot * 1024:qslot * 1024 + 512],
                                op0=ALU.mult, op1=ALU.add))
                            eng.wait_ge(chv.h, chv.n)
                            avE.inc(eng.scalar_tensor_tensor(
                                out=oacc[:, qslot * 1024 + 512
                                         :(qslot + 1) * 1024],
                                in0=pmA[1][:, :], scalar=1.0,
                                in1=oacc[:, qslot * 1024 + 512
                                         :(qslot + 1) * 1024],
                                op0=ALU.mult, op1=ALU.add))
                            eng.wait_ge(chv.h, chv.n)
                            avE.inc(eng.scalar_tensor_tensor(
                                out=sacc[:, qslot:qslot + 1], in0=pmX[:, 0:1],
                                scalar=1.0, in1=sacc[:, qslot:qslot + 1],
                                op0=ALU.mult, op1=ALU.add))
                            eng.wait_ge(chv.h, chv.n)
                av_pred += 3 * QTPG
                esb_free_mmA = mmA_rec[(b, kq, qg, QTPG - 1)]

        for qt in range(NQT):
            ti_y = b * NQT + qt

            @blk.sync
            def _(eng, b=b, qt=qt, ti_y=ti_y):
                if ti_y >= 2:
                    eng.wait_ge(epi.h, ti_y - 1)
                xrl[ti_y % 2].inc(eng.dma_start(
                    out=xrb[ti_y % 2][:, :],
                    in_=xq_in[b, qt * 128:(qt + 1) * 128, :]), 16)
                xrl_rec[ti_y] = xrl[ti_y % 2].n

            @blk.vector
            def _(eng, b=b, qt=qt, ti_y=ti_y, avp0=avE.n):
                eng.wait_ge(avE.h, avp0)
                chv.inc(eng.reciprocal(out=statL[:, 4:5],
                                       in_=sacc[:, qt:qt + 1]))
                eng.wait_ge(chv.h, chv.n)
                chv.inc(eng.tensor_scalar_mul(
                    out=oacc[:, qt * 1024:(qt + 1) * 1024],
                    in0=oacc[:, qt * 1024:(qt + 1) * 1024],
                    scalar1=statL[:, 4:5]))
                eng.wait_ge(chv.h, chv.n)
                if ti_y >= 2:
                    eng.wait_ge(tre.h, ofm_ev[ti_y - 2])
                chv.inc(eng.tensor_copy(out=cstL[ti_y % 2][:, :],
                                        in_=oacc[:, qt * 1024:(qt + 1) * 1024]))
                ocast_rec[ti_y] = chv.n

            @blk.tensor
            def _(eng, b=b, qt=qt, ti_y=ti_y):
                eng.wait_ge(chv.h, ocast_rec[ti_y])
                if ti_y == 0:
                    eng.wait_ge(wcl.h, 32)
                for half in range(2):
                    h = cnt["half"]
                    cnt["half"] += 1
                    if h >= 2:
                        eng.wait_ge(tre.h, h - 1)
                    for bi in range(4):
                        trc.inc(eng.transpose(
                            pxt[h % 2][:, bi * 128:(bi + 1) * 128],
                            cstL[ti_y % 2][:, half * 512 + bi * 128
                                           :half * 512 + (bi + 1) * 128],
                            ident_sb[:, :]))
                    half_done[h] = (trc.n, 0, half)
                ofm_tr[ti_y] = (cnt["half"] - 2, cnt["half"] - 1)

            @blk.scalar
            def _(eng, b=b, qt=qt, ti_y=ti_y):
                for hh, h in enumerate(ofm_tr[ti_y]):
                    trcv, _, hf = half_done[h]
                    eng.wait_ge(trc.h, trcv)
                    if hh == 0 and ti_y >= 1:
                        eng.wait_ge(mmA.h, mmA.n)
                    tre.inc(eng.copy(
                        out=ofmL[:, hf * 512:(hf + 1) * 512],
                        in_=pxt[h % 2][:, :]))
                ofm_ev[ti_y] = tre.n

            @blk.tensor
            def _(eng, b=b, qt=qt, ti_y=ti_y, avp0=avE.n):
                eng.wait_ge(tre.h, ofm_ev[ti_y])
                eng.wait_ge(avE.h, avp0)
                for og in range(2):
                    for ic in range(8):
                        inst = eng.matmul(
                            pmA[og][:, :],
                            ofmL[:, ic * 128:(ic + 1) * 128],
                            wcs[:, og * 4096 + ic * 512
                                :og * 4096 + (ic + 1) * 512],
                            start=(ic == 0), stop=(ic == 7))
                    mmA.inc(inst)
                wc_mm_rec[ti_y] = mmA.n

            @blk.vector
            def _(eng, b=b, qt=qt, ti_y=ti_y):
                eng.wait_ge(mmA.h, wc_mm_rec[ti_y])
                eng.wait_ge(xrl[ti_y % 2].h, xrl_rec[ti_y])
                if ti_y >= 2:
                    eng.wait_ge(yst[ti_y % 2].h, 16 * ((ti_y - 2) // 2 + 1))
                for og in range(2):
                    avE.inc(eng.scalar_tensor_tensor(
                        out=yb[ti_y % 2][:, og * 512:(og + 1) * 512],
                        in0=pmA[og][:, :], scalar=1.0,
                        in1=bcb[:, og * 512:(og + 1) * 512],
                        op0=ALU.mult, op1=ALU.add))
                    eng.wait_ge(avE.h, avE.n)
                wc_ev_rec[ti_y] = avE.n
                chv.inc(eng.tensor_add(out=yb[ti_y % 2][:, :],
                                       in0=yb[ti_y % 2][:, :],
                                       in1=xrb[ti_y % 2][:, :]))
                eng.wait_ge(chv.h, chv.n)
                chv.inc(eng.reduce_sum(out=statL[:, 5:6], in_=yb[ti_y % 2][:, :],
                                       axis=AX.X))
                eng.wait_ge(chv.h, chv.n)
                chv.inc(eng.tensor_scalar_mul(out=statL[:, 6:7],
                                              in0=statL[:, 5:6],
                                              scalar1=1.0 / 1024.0))
                eng.wait_ge(chv.h, chv.n)
                chv.inc(eng.tensor_scalar_sub(out=yb[ti_y % 2][:, :],
                                              in0=yb[ti_y % 2][:, :],
                                              scalar1=statL[:, 6:7]))
                ln_c_rec[ti_y] = chv.n

            @blk.scalar
            def _(eng, b=b, qt=qt, ti_y=ti_y):
                eng.wait_ge(chv.h, ln_c_rec[ti_y])
                eng.wait_ge(chs.h, chs.n)
                chs.inc(eng.activation(out=trashE[:, :],
                                       in_=yb[ti_y % 2][:, :], func=AF.Square,
                                       accum_out=statL[:, 7:8]))
                eng.wait_ge(chs.h, chs.n)
                chs.inc(eng.activation(out=statL[:, 8:9], in_=statL[:, 7:8],
                                       func=AF.Sqrt, scale=1.0 / 1024.0,
                                       bias=epsl[:, 0:1]))
                ln_s_rec[ti_y] = chs.n

            @blk.vector
            def _(eng, b=b, qt=qt, ti_y=ti_y):
                eng.wait_ge(chs.h, ln_s_rec[ti_y])
                chv.inc(eng.reciprocal(out=statL[:, 9:10], in_=statL[:, 8:9]))
                eng.wait_ge(chv.h, chv.n)
                epi.inc(eng.tensor_scalar_mul(out=yb[ti_y % 2][:, :],
                                              in0=yb[ti_y % 2][:, :],
                                              scalar1=statL[:, 9:10]))

            @blk.gpsimd
            def _(eng, b=b, qt=qt, ti_y=ti_y):
                eng.wait_ge(epi.h, ti_y + 1)
                yst[ti_y % 2].inc(eng.dma_start(
                    out=y_out[ti_y * 128:(ti_y + 1) * 128, :],
                    in_=yb[ti_y % 2][:, :]), 16)

    @blk.gpsimd
    def _(eng):
        eng.wait_ge(yst[0].h, 16 * ((NT_Y + 1) // 2))
        eng.wait_ge(yst[1].h, 16 * (NT_Y // 2))

    p.close()
    return p


def run_l0a(inputs, x_bsd=None):
    p = build_l0a()
    x = inputs["L0_attn_x"] if x_bsd is None else x_bsd
    idn = ident_np()
    maps = []
    for c in range(NCORES):
        maps.append({
            "x": np.ascontiguousarray(x),
            "xq": np.ascontiguousarray(x[:, c * POS:(c + 1) * POS, :]),
            "theta": inputs["L0_attn_theta"],
            "W_c": inputs["L0_attn_W_c"],
            "b_c": inputs["L0_attn_b_c"],
            "ident": idn,
        })
    res = run_bass_kernel_spmd(p.nc, maps, core_ids=list(range(NCORES)))
    y = np.stack([res.results[c]["y"] for c in range(NCORES)])
    return tokens_to_bsd(y)



def build_l0f(final=False, nb=B):
    NIT = 18
    KT0 = 2
    NKT = 16
    IQ0 = 7
    NQT2 = 9
    NOT_ = 8
    HID = 2730
    W1OS = [512] * 10 + [340]
    NOS = len(W1OS)
    NW2C = 22
    p = Prog()
    xh = p.inp("xh", [nb, NIT * 128, D])
    x1o = p.inp("x1o", [nb, NOT_ * 128, D]) if final else None
    wqkv = p.inp("wqkv", [1024, 1536])
    qs_in = p.inp("q_scale", [64]); ks_in = p.inp("k_scale", [64])
    wo_in = p.inp("wo", [512, 1024])
    lg_in = p.inp("ln_g", [1024]); lb_in = p.inp("ln_b", [1024])
    w1_in = p.inp("w1", [1024, 5460]); b1_in = p.inp("b1", [5460])
    w2_in = p.inp("w2", [HID, 1024]); b2_in = p.inp("b2", [1024])
    rq_in = p.inp("rq", [NKT * 128, 1024])
    rk_in = p.inp("rk", [NKT * 128, 1024])
    msk = p.inp("msk", [NQT2, 128, 1024], BF16)
    rz_in = p.inp("rz", [128, 16])
    idn = p.inp("ident", [128, 128], BF16)
    y_out = p.out("y", [nb * NOT_ * 128, D])

    wqc = p.dram("wqc", [1024, 1536], BF16)
    woc = p.dram("woc", [512, 1024], BF16)
    w1c = p.dram("w1c", [1024, 5460], BF16)
    w2c = p.dram("w2c", [HID, 1024], BF16)
    b1c = p.dram("b1c", [5460], BF16)
    b2c = p.dram("b2c", [1024], BF16)
    xm = p.dram("xm", [NQT2 * 128, D])

    ident_sb = p.sb("identb", [128, 128], BF16)
    arena = p.sb("arena", [128, 25600], BF16)
    kfm = arena[:, 0:8192]
    vst = arena[:, 8192:16384]
    qfm = arena[:, 16384:25600]
    ab_all = arena[:, 0:21840]
    wqs = p.sb("wqsF", [128, 8 * 1536], BF16)
    wos = p.sb("wosF", [128, 4 * 1024], BF16)
    w1b = [p.sb(f"w1bF{i}", [128, 8 * 512], BF16) for i in range(2)]
    w2b = [p.sb(f"w2bF{i}", [128, 1024], BF16) for i in range(2)]
    xfm_own = p.sb("xfm_ownF", [128, 8192], BF16)
    xfm32 = xfm_own.bitcast(F32)
    wcst = [xfm32[:, 0:1024], xfm32[:, 1024:2048]]
    wcsb = [arena[:, 0:2048], arena[:, 2048:4096]]
    hfm2 = p.sb("hfm2F", [128, NW2C * 128], BF16)
    hfm_sb = p.sb("hfmF", [128, 1024], BF16)
    hb = [p.sb(f"hbF{i}", [128, 1024], F32) for i in range(2)]
    cst = [p.sb(f"cstF{i}", [128, 1024], BF16) for i in range(2)]
    qkc = p.sb("qkcF", [128, 512], BF16)
    qw = [p.sb(f"qwF{i}", [128, 512], F32) for i in range(2)]
    kw = [p.sb(f"kwF{i}", [128, 512], F32) for i in range(2)]
    sq2 = p.sb("sq2F", [128, 512], F32)
    rot = sq2
    rtk = p.sb("rtkF", [128, 1024], F32)
    rtq = p.sb("rtqF", [128, 1024], F32)
    qsb = p.sb("qsbF", [128, 512], F32)
    ksb = p.sb("ksbF", [128, 512], F32)
    lgb = p.sb("lgbF", [128, 1024], F32)
    lbb = p.sb("lbbF", [128, 1024], F32)
    msb = p.sb("msbF", [128, 1024], BF16)
    esb = [p.sb(f"esbF{i}", [128, 1024], BF16) for i in range(2)]
    etf = p.sb("etfF", [128, 1024], BF16)
    ofm = p.sb("ofmF", [128, 512], BF16)
    sums = p.sb("sumsF", [128, 16], F32)
    statF = p.sb("statF2", [128, 32], F32)
    gt = [p.sb(f"gtF{i}", [128, 1024], F32) for i in range(2)]
    gtb = [g.bitcast(BF16) for g in gt]
    xmb = [p.sb(f"xmbF{i}", [128, 1024], F32) for i in range(2)]
    ybF = [p.sb(f"ybFF{i}", [128, 1024], F32) for i in range(2)]
    x1b2 = gt[0]
    rowst = xmb[0][0:1, 0:512]
    brow = [p.sb(f"browF{i}", [1, 512], BF16) for i in range(2)]
    brow2 = [p.sb(f"brw2F{i}", [1, 512], BF16) for i in range(2)]
    rzb = p.sb("rzbF", [128, 16], F32)
    epsl = p.sb("epslF", [128, 1], F32)
    onesr = p.sb("onesrF", [1, 128], F32)
    ones_bf = p.sb("onesbF", [1, 128], BF16)

    pmQ = [p.ps(f"pmQF{i}", [128, 512], F32) for i in range(2)]
    pmS2 = [p.ps(f"pmSF{i}", [128, 512], F32) for i in range(4)]
    pmAV = p.ps("pmAVF", [64, 128], F32)
    pxtF = p.ps("pxtFb", [128, 512], BF16)

    blk = p.block()

    ild = p.sem("ild")
    cns = p.sem("cns")
    chv = p.sem("chv")
    chs = p.sem("chs")
    mmk = p.sem("mmk")
    mm2 = p.sem("mm2")
    trc = p.sem("trc")
    tre = p.sem("tre")
    ev2 = p.sem("ev2")
    cht = p.sem("cht")
    wld2 = [p.sem(f"wldF{i}") for i in range(2)]
    wst2 = [p.sem(f"wstF{i}") for i in range(2)]
    bst = p.sem("bstF")
    wqr = p.sem("wqrF")
    xl = [p.sem(f"xlF{i}") for i in range(2)]
    rlk = p.sem("rlkF")
    rlq = p.sem("rlqF")
    ml = p.sem("mlF")
    xrl = [p.sem(f"xrlF{i}") for i in range(2)]
    xml = [p.sem(f"xmlF{i}") for i in range(2)]
    xle = [p.sem(f"xleF{i}") for i in range(2)]
    x1l = p.sem("x1lF")
    w1l = [p.sem(f"w1lF{i}") for i in range(2)]
    w2l = [p.sem(f"w2lF{i}") for i in range(2)]
    xms = [p.sem(f"xmsF{i}") for i in range(2)]
    yst = [p.sem(f"ystF{i}") for i in range(2)]

    rec = {}
    evt = [0]
    hb_hist = []
    cst_hist = []

    def VOP(eng, mk):
        eng.wait_ge(chv.h, chv.n)
        chv.inc(mk())
        return chv.n

    def SOP(eng, mk):
        eng.wait_ge(chs.h, chs.n)
        chs.inc(mk())
        return chs.n

    kfm3 = kfm.rearrange("p (c w) -> p c w", w=2048)
    qfm3 = qfm.rearrange("p (c w) -> p c w", w=1152)
    ab3 = ab_all.rearrange("p (t c) -> p t c", c=HID)
    qkv_gi = [0]

    @blk.sync
    def _(eng):
        ild.inc(eng.dma_start(out=ident_sb[:, :], in_=idn), 16)
        ild.inc(eng.dma_start(out=rowst[0:1, 0:64], in_=qs_in.unsqueeze(0)), 16)
        ild.inc(eng.dma_start(out=rowst[0:1, 64:128],
                              in_=ks_in.unsqueeze(0)), 16)
    ILD0 = ild.n

    @blk.vector
    def _(eng):
        cns.inc(eng.memset(epsl[:, :], 1e-5))
        cns.inc(eng.memset(onesr[:, :], 1.0))
        cns.inc(eng.memset(ones_bf[:, :], 1.0))
        cns.inc(eng.memset(qfm, 0.0))
    CNS0 = cns.n

    @blk.tensor
    def _(eng):
        eng.wait_ge(ild.h, ILD0)
        eng.wait_ge(cns.h, CNS0)
        for h8 in range(8):
            cht.inc(eng.matmul(pmQ[0][:, h8 * 64:(h8 + 1) * 64],
                               onesr[0:1, :], rowst[0:1, 0:64],
                               start=True, stop=True))
        for h8 in range(8):
            cht.inc(eng.matmul(pmQ[1][:, h8 * 64:(h8 + 1) * 64],
                               onesr[0:1, :], rowst[0:1, 64:128],
                               start=True, stop=True))
    CHT0 = cht.n

    @blk.scalar
    def _(eng):
        eng.wait_ge(cht.h, CHT0)
        SOP(eng, lambda: eng.copy(out=qsb[:, :], in_=pmQ[0][:, :]))
        SOP(eng, lambda: eng.copy(out=ksb[:, :], in_=pmQ[1][:, :]))
    CHS0 = chs.n

    lgjobs = [(lg_in, lgb, 0), (lg_in, lgb, 512), (lb_in, lbb, 0),
              (lb_in, lbb, 512)]
    lg_ld = []
    lg_mm = [CHT0 + j + 1 for j in range(4)]
    lg_cp = [CHS0 + j + 1 for j in range(4)]

    @blk.sync
    def _(eng):
        for j, (src, dst, off) in enumerate(lgjobs):
            if j == 0:
                eng.wait_ge(chs.h, CHS0)
            else:
                eng.wait_ge(cht.h, lg_mm[j - 1])
            ild.inc(eng.dma_start(out=rowst[0:1, :],
                                  in_=src[off:off + 512].unsqueeze(0)), 16)
            lg_ld.append(ild.n)

    @blk.tensor
    def _(eng):
        for j, (src, dst, off) in enumerate(lgjobs):
            eng.wait_ge(ild.h, lg_ld[j])
            if j >= 1:
                eng.wait_ge(chs.h, lg_cp[j - 1])
            cht.inc(eng.matmul(pmQ[0][:, :], onesr[0:1, :], rowst[0:1, :],
                               start=True, stop=True))

    @blk.scalar
    def _(eng):
        for j, (src, dst, off) in enumerate(lgjobs):
            eng.wait_ge(cht.h, lg_mm[j])
            SOP(eng, lambda dst=dst, off=off: eng.copy(
                out=dst[:, off:off + 512], in_=pmQ[0][:, :]))
    CHS_PRO = chs.n

    bspec = ([(b1_in, b1c, k * 512, min(512, 5460 - k * 512))
              for k in range(11)] +
             [(b2_in, b2c, 0, 512), (b2_in, b2c, 512, 512)])
    bl_rec = []
    bc_rec = []

    @blk.sync
    def _(eng):
        for k, (src, dst, off, w) in enumerate(bspec):
            if k == 0:
                eng.wait_ge(cht.h, lg_mm[-1])
            else:
                eng.wait_ge(cns.h, CNS0 + k)
            ild.inc(eng.dma_start(out=rowst[0:1, 0:w],
                                  in_=src[off:off + w].unsqueeze(0)), 16)
            bl_rec.append(ild.n)

    @blk.vector
    def _(eng):
        for k, (src, dst, off, w) in enumerate(bspec):
            eng.wait_ge(ild.h, bl_rec[k])
            if k >= 1:
                eng.wait_ge(bst.h, 16 * k)
            cns.inc(eng.tensor_copy(out=brow[0][0:1, 0:w],
                                    in_=rowst[0:1, 0:w]))
            bc_rec.append(cns.n)

    @blk.gpsimd
    def _(eng):
        for k, (src, dst, off, w) in enumerate(bspec):
            eng.wait_ge(cns.h, bc_rec[k])
            bst.inc(eng.dma_start(out=dst[off:off + w].unsqueeze(0),
                                  in_=brow[0][0:1, 0:w]), 16)

    wjobs = []
    for (win, wcd, no, ni) in ((wqkv, wqc, 1536, 1024), (wo_in, woc, 1024, 512),
                               (w1_in, w1c, 5460, 1024),
                               (w2_in, w2c, 1024, HID)):
        for r0 in range(0, ni, 128):
            pr = min(128, ni - r0)
            for c0 in range(0, no, 1024):
                cw = min(1024, no - c0)
                wjobs.append((win, wcd, r0, pr, c0, cw))
    wl_rec2 = []
    wcv_rec = []
    CNS_W0 = CNS0 + len(bspec)

    @blk.sync
    def _(eng):
        for i, (win, wcd, r0, pr, c0, cw) in enumerate(wjobs):
            if i < 2:
                eng.wait_ge(cns.h, CNS_W0)
            else:
                eng.wait_ge(cns.h, CNS_W0 + i - 1)
            wld2[i % 2].inc(eng.dma_start(
                out=wcst[i % 2][0:pr, 0:cw],
                in_=win[r0:r0 + pr, c0:c0 + cw]), 16)
            wl_rec2.append(wld2[i % 2].n)

    @blk.vector
    def _(eng):
        for i, (win, wcd, r0, pr, c0, cw) in enumerate(wjobs):
            eng.wait_ge(wld2[i % 2].h, wl_rec2[i])
            if i >= 2:
                eng.wait_ge(wst2[i % 2].h, 16 * (i // 2))
            cns.inc(eng.tensor_copy(out=wcsb[i % 2][0:pr, 0:cw],
                                    in_=wcst[i % 2][0:pr, 0:cw]))
            wcv_rec.append(cns.n)

    @blk.gpsimd
    def _(eng):
        for i, (win, wcd, r0, pr, c0, cw) in enumerate(wjobs):
            eng.wait_ge(cns.h, wcv_rec[i])
            wst2[i % 2].inc(eng.dma_start(
                out=wcd[r0:r0 + pr, c0:c0 + cw],
                in_=wcsb[i % 2][0:pr, 0:cw]), 16)
    WST_ALL = (wst2[0].n, wst2[1].n)

    @blk.sync
    def _(eng):
        eng.wait_ge(wst2[0].h, WST_ALL[0])
        eng.wait_ge(wst2[1].h, WST_ALL[1])
        wqr.inc(eng.dma_start(
            out=wqs[:, :].rearrange("p (c o) -> p c o", c=8),
            in_=wqc.rearrange("(c p) o -> p c o", p=128)), 16)
        wqr.inc(eng.dma_start(
            out=wos[:, :].rearrange("p (c o) -> p c o", c=4),
            in_=woc.rearrange("(c p) o -> p c o", p=128)), 16)
        wqr.inc(eng.dma_start(out=rzb[:, :], in_=rz_in), 16)
    WQR = wqr.n

    for b in range(nb):
        if b > 0:
            @blk.vector
            def _(eng, b=b):
                eng.wait_ge(trc.h, rec[("htr", b - 1, NOT_ - 1)])
                VOP(eng, lambda: eng.memset(arena[:, 16384:21840], 0.0))
                rec[("qzero", b)] = chv.n

        for i in range(NKT):
            it = KT0 + i
            isq = i >= IQ0
            slot = len(hb_hist)

            @blk.sync
            def _(eng, b=b, i=i, it=it, slot=slot, isq=isq):
                if slot >= 2:
                    eng.wait_ge(chv.h, hb_hist[slot - 2])
                xl[i % 2].inc(eng.dma_start(
                    out=hb[i % 2][:, 0:512],
                    in_=xh[b, it * 128:(it + 1) * 128, 0:512]), 16)
                xl[i % 2].inc(eng.dma_start(
                    out=hb[i % 2][:, 512:1024],
                    in_=xh[b, it * 128 - 1:(it + 1) * 128 - 1, 512:1024]), 16)
                rec[("xl", b, i)] = xl[i % 2].n
                prevk = rec.get(("ktab", b, i - 1),
                                rec.get(("ktab", b - 1, NKT - 1), 0))
                eng.wait_ge(chv.h, prevk)
                rlk.inc(eng.dma_start(out=rtk[:, :],
                                      in_=rk_in[i * 128:(i + 1) * 128, :]), 16)
                rec[("rlk", b, i)] = rlk.n
                if isq:
                    prevq = rec.get(("qtab", b, i - 1),
                                    rec.get(("qtab", b - 1, NKT - 1), 0))
                    eng.wait_ge(chv.h, prevq)
                    rlq.inc(eng.dma_start(
                        out=rtq[:, :], in_=rq_in[i * 128:(i + 1) * 128, :]), 16)
                    rec[("rlq", b, i)] = rlq.n

            @blk.vector
            def _(eng, b=b, i=i, slot=slot):
                eng.wait_ge(xl[i % 2].h, rec[("xl", b, i)])
                if b == 0 and i == 0:
                    eng.wait_ge(chs.h, CHS_PRO)
                if slot >= 2:
                    eng.wait_ge(trc.h, cst_hist[slot - 2])
                VOP(eng, lambda: eng.tensor_copy(out=cst[i % 2][:, :],
                                                 in_=hb[i % 2][:, :]))
                rec[("hcast", b, i)] = chv.n
                hb_hist.append(chv.n)

            @blk.tensor
            def _(eng, b=b, i=i):
                eng.wait_ge(chv.h, rec[("hcast", b, i)])
                eng.wait_ge(mmk.h, rec.get(("qkv_last", b, i - 1),
                                           rec.get(("qkv_last", b - 1,
                                                    NKT - 1), 0)))
                for half in range(2):
                    if evt[0] > 0:
                        eng.wait_ge(tre.h, evt[0])
                    for bi in range(4):
                        trc.inc(eng.transpose(
                            pxtF[:, bi * 128:(bi + 1) * 128],
                            cst[i % 2][:, half * 512 + bi * 128
                                       :half * 512 + (bi + 1) * 128],
                            ident_sb[:, :]))
                    rec[("htp", b, i, half)] = trc.n
                    evt[0] += 1
                cst_hist.append(trc.n)

            @blk.scalar
            def _(eng, b=b, i=i):
                for half in range(2):
                    eng.wait_ge(trc.h, rec[("htp", b, i, half)])
                    tre.inc(eng.copy(
                        out=hfm_sb[:, half * 512:(half + 1) * 512],
                        in_=pxtF[:, :]))
                rec[("hfm", b, i)] = tre.n

            ogl = ([0] if isq else []) + [1, 2]

            for og in ogl:
                @blk.tensor
                def _(eng, b=b, i=i, og=og, first=(og == ogl[0])):
                    if first:
                        eng.wait_ge(tre.h, rec[("hfm", b, i)])
                    if b == 0 and i == 0 and first:
                        eng.wait_ge(wqr.h, WQR)
                    gi = qkv_gi[0]
                    if rec.get(("qkve_g", gi - 2)) is not None:
                        eng.wait_ge(ev2.h, rec[("qkve_g", gi - 2)])
                    for ic in range(8):
                        inst = eng.matmul(
                            pmQ[gi % 2][:, :],
                            hfm_sb[:, ic * 128:(ic + 1) * 128],
                            wqs[:, ic * 1536 + og * 512
                                :ic * 1536 + (og + 1) * 512],
                            start=(ic == 0), stop=(ic == 7))
                    mmk.inc(inst)
                    rec[("qkv", b, i, og)] = mmk.n
                    rec[("qkv_bank", b, i, og)] = gi % 2
                    rec[("qkv_gi", b, i, og)] = gi
                    qkv_gi[0] += 1
                    rec[("qkv_last", b, i)] = mmk.n

                @blk.scalar
                def _(eng, b=b, i=i, og=og):
                    eng.wait_ge(mmk.h, rec[("qkv", b, i, og)])
                    bank = rec[("qkv_bank", b, i, og)]
                    if og == 0:
                        prev = rec.get(("qdone", b, i - 2),
                                       rec.get(("qdone", b - 1, NKT - 1), 0))
                        eng.wait_ge(chv.h, prev)
                        ev2.inc(eng.copy(out=qw[i % 2][:, :],
                                         in_=pmQ[bank][:, :]))
                        rec[("qev", b, i)] = ev2.n
                    elif og == 1:
                        prev = rec.get(("kdone", b, i - 2),
                                       rec.get(("kdone", b - 1, NKT - 2 + i),
                                               0))
                        eng.wait_ge(chv.h, prev)
                        ev2.inc(eng.copy(out=kw[i % 2][:, :],
                                         in_=pmQ[bank][:, :]))
                        rec[("kev", b, i)] = ev2.n
                    else:
                        if b > 0:
                            eng.wait_ge(trc.h, rec[("htr", b - 1, NOT_ - 1)])
                        ev2.inc(eng.copy(out=vst[:, i * 512:(i + 1) * 512],
                                         in_=pmQ[bank][:, :]))
                    rec[("qkve_g", rec[("qkv_gi", b, i, og)])] = ev2.n
                    rec[("qkve", b, i)] = ev2.n

            for which in (["k", "q"] if isq else ["k"]):
                wv = kw if which == "k" else qw
                rt = rtk if which == "k" else rtq
                scb = ksb if which == "k" else qsb

                @blk.vector
                def _(eng, b=b, i=i, which=which, wv=wv):
                    eng.wait_ge(ev2.h,
                                rec[("kev" if which == "k" else "qev", b, i)])
                    eng.wait_ge((rlk if which == "k" else rlq).h,
                                rec[("rlk" if which == "k" else "rlq", b, i)])
                    w = wv[i % 2]
                    VOP(eng, lambda: eng.scalar_tensor_tensor(
                        out=sq2[:, :], in0=w[:, :], scalar=1.0, in1=w[:, :],
                        op0=ALU.mult, op1=ALU.mult))
                    VOP(eng, lambda: eng.tensor_reduce(
                        out=statF[:, 0:8],
                        in_=sq2[:, :].rearrange("p (h d) -> p h d", h=8),
                        axis=AX.X, op=ALU.add))
                    rec[("ssq", b, i, which)] = chv.n

                @blk.scalar
                def _(eng, b=b, i=i, which=which):
                    eng.wait_ge(chv.h, rec[("ssq", b, i, which)])
                    SOP(eng, lambda: eng.activation(
                        out=statF[:, 8:16], in_=statF[:, 0:8], func=AF.Sqrt))
                    rec[("nsqrt", b, i, which)] = chs.n

                @blk.vector
                def _(eng, b=b, i=i, which=which, wv=wv, rt=rt, scb=scb):
                    eng.wait_ge(chs.h, rec[("nsqrt", b, i, which)])
                    w = wv[i % 2]
                    w3 = w[:, :].rearrange("p (h d) -> p h d", h=8)
                    r3 = rot[:, :].rearrange("p (h d) -> p h d", h=8)
                    VOP(eng, lambda: eng.tensor_scalar(
                        out=statF[:, 8:16], in0=statF[:, 8:16], scalar1=1e-12,
                        scalar2=None, op0=ALU.max))
                    VOP(eng, lambda: eng.reciprocal(out=statF[:, 16:24],
                                                    in_=statF[:, 8:16]))
                    VOP(eng, lambda: eng.tensor_tensor(
                        out=w3, in0=w3,
                        in1=statF[:, 16:24].unsqueeze(2)
                        .broadcast_to([128, 8, 64]), op=ALU.mult))
                    VOP(eng, lambda: eng.tensor_tensor(
                        out=w[:, :], in0=w[:, :], in1=scb[:, :], op=ALU.mult))
                    VOP(eng, lambda: eng.tensor_tensor(
                        out=r3[:, :, 0:32], in0=w3[:, :, 32:64],
                        in1=rt[:, 512:768].rearrange("p (h d) -> p h d", h=8),
                        op=ALU.mult))
                    VOP(eng, lambda: eng.tensor_tensor(
                        out=r3[:, :, 32:64], in0=w3[:, :, 0:32],
                        in1=rt[:, 768:1024].rearrange("p (h d) -> p h d", h=8),
                        op=ALU.mult))
                    VOP(eng, lambda: eng.tensor_tensor(
                        out=w[:, :], in0=w[:, :], in1=rt[:, 0:512],
                        op=ALU.mult))
                    rec[("ktab" if which == "k" else "qtab", b, i)] = chv.n
                    VOP(eng, lambda: eng.tensor_tensor(
                        out=w[:, :], in0=w[:, :], in1=rot[:, :], op=ALU.add))
                    prev = rec.get(("qkc_prev",))
                    if prev is not None:
                        eng.wait_ge(trc.h, prev)
                    VOP(eng, lambda: eng.tensor_copy(out=qkc[:, :],
                                                     in_=w[:, :]))
                    rec[("qkcast", b, i, which)] = chv.n
                    rec[("qdone" if which == "q" else "kdone", b, i)] = chv.n

                @blk.tensor
                def _(eng, b=b, i=i, which=which):
                    eng.wait_ge(chv.h, rec[("qkcast", b, i, which)])
                    if evt[0] > 0:
                        eng.wait_ge(tre.h, evt[0])
                    for bi in range(4):
                        trc.inc(eng.transpose(
                            pxtF[:, bi * 128:(bi + 1) * 128],
                            qkc[:, bi * 128:(bi + 1) * 128],
                            ident_sb[:, :]))
                    rec[("ftr", b, i, which)] = trc.n
                    rec[("qkc_prev",)] = trc.n
                    evt[0] += 1 if which == "k" else 8

                @blk.scalar
                def _(eng, b=b, i=i, which=which):
                    eng.wait_ge(trc.h, rec[("ftr", b, i, which)])
                    if which == "k":
                        tre.inc(eng.copy(
                            out=kfm3[:, :, i * 128:(i + 1) * 128],
                            in_=pxtF[:, :].rearrange("p (c w) -> p c w",
                                                     w=128)))
                    else:
                        if i == IQ0 and b > 0:
                            eng.wait_ge(chv.h, rec[("qzero", b)])
                        iq = i - IQ0
                        for bi in range(4):
                            for hh in range(2):
                                h = 2 * bi + hh
                                tre.inc(eng.copy(
                                    out=qfm3[hh * 64:(hh + 1) * 64, h,
                                             iq * 128:(iq + 1) * 128],
                                    in_=pxtF[hh * 64:(hh + 1) * 64,
                                             bi * 128:(bi + 1) * 128]))
                    rec[("ffm", b, i, which)] = tre.n

        for iq in range(NQT2):
            kst = ((iq + 3) // 4) * 4

            @blk.sync
            def _(eng, b=b, iq=iq):
                prev = rec.get(("simlast", b, iq - 1),
                               rec.get(("simlast", b - 1, NQT2 - 1), 0))
                eng.wait_ge(mm2.h, prev)
                ml.inc(eng.dma_start(out=msb[:, :], in_=msk[iq, :, :]), 16)
                rec[("ml", b, iq)] = ml.n

            for hp in range(4):
                @blk.tensor
                def _(eng, b=b, iq=iq, hp=hp, kst=kst):
                    eng.wait_ge(ml.h, rec[("ml", b, iq)])
                    eng.wait_ge(tre.h, rec[("ffm", b, NKT - 1, "q")])
                    if hp == 0:
                        if rec.get(("xme", b, iq - 1)) is not None:
                            eng.wait_ge(chv.h, rec[("xme", b, iq - 1)])
                        elif b > 0 and iq == 0:
                            eng.wait_ge(chv.h, rec[("yc", b - 1, NOT_ - 1)])
                        prev = rec.get(("sime", b, iq - 1, 3),
                                       rec.get(("sime", b - 1, NQT2 - 1, 3),
                                               0))
                    else:
                        prev = rec[("sime", b, iq, hp - 1)]
                    eng.wait_ge(ev2.h, prev)
                    for hh in range(2):
                        h = 2 * hp + hh
                        for kk in range(2):
                            eng.matmul(
                                pmS2[hh * 2 + kk][:, :],
                                qfm3[:, h, iq * 128:(iq + 1) * 128],
                                kfm3[:, hp, (kst + 4 * kk) * 128
                                     :(kst + 4 * kk + 4) * 128],
                                start=True, stop=False)
                            inst = eng.matmul(
                                pmS2[hh * 2 + kk][:, :],
                                ident_sb[:, :],
                                msb[:, kk * 512:(kk + 1) * 512],
                                start=False, stop=True)
                            mm2.inc(inst)
                            rec[("sim", b, iq, hp, hh, kk)] = mm2.n
                    rec[("simlast", b, iq)] = mm2.n

                @blk.vector
                def _(eng, b=b, iq=iq, hp=hp):
                    for hh in range(2):
                        eng.wait_ge(mm2.h, rec[("sim", b, iq, hp, hh, 1)])
                        VOP(eng, lambda hh=hh: eng.tensor_reduce(
                            out=statF[:, 0:1], in_=pmS2[hh * 2][:, :],
                            axis=AX.X, op=ALU.max))
                        VOP(eng, lambda hh=hh: eng.tensor_reduce(
                            out=statF[:, 1:2], in_=pmS2[hh * 2 + 1][:, :],
                            axis=AX.X, op=ALU.max))
                        VOP(eng, lambda hh=hh: eng.tensor_tensor(
                            out=statF[:, 2 + hh:3 + hh], in0=statF[:, 0:1],
                            in1=statF[:, 1:2], op=ALU.max))
                        VOP(eng, lambda hh=hh: eng.tensor_scalar(
                            out=statF[:, 4 + hh:5 + hh],
                            in0=statF[:, 2 + hh:3 + hh],
                            scalar1=-8.0, scalar2=None, op0=ALU.mult))
                    rec[("rmax", b, iq, hp)] = chv.n

                @blk.scalar
                def _(eng, b=b, iq=iq, hp=hp):
                    eng.wait_ge(chv.h, rec[("rmax", b, iq, hp)])
                    for hh in range(2):
                        prevt = rec.get(("etr_done", b, iq, hp - 1, hh),
                                        rec.get(("etr_done", b, iq - 1, 3, hh),
                                                rec.get(("etr_done", b - 1,
                                                         NQT2 - 1, 3, hh), 0)))
                        eng.wait_ge(trc.h, prevt)
                        prevc = rec.get(("scomb", b, iq, hp - 1),
                                        rec.get(("scomb", b, iq - 1, 3),
                                                rec.get(("scomb", b - 1,
                                                         NQT2 - 1, 3), 0)))
                        eng.wait_ge(chv.h, prevc)
                        ev2.inc(eng.activation(
                            out=esb[hh][:, 0:512], in_=pmS2[hh * 2][:, :],
                            func=AF.Exp, scale=8.0,
                            bias=statF[:, 4 + hh:5 + hh],
                            accum_out=sums[:, 8 + 2 * hh:9 + 2 * hh]))
                        ev2.inc(eng.activation(
                            out=esb[hh][:, 512:1024],
                            in_=pmS2[hh * 2 + 1][:, :],
                            func=AF.Exp, scale=8.0,
                            bias=statF[:, 4 + hh:5 + hh],
                            accum_out=sums[:, 9 + 2 * hh:10 + 2 * hh]))
                    rec[("sime", b, iq, hp)] = ev2.n

                @blk.vector
                def _(eng, b=b, iq=iq, hp=hp):
                    eng.wait_ge(ev2.h, rec[("sime", b, iq, hp)])
                    for hh in range(2):
                        h = 2 * hp + hh
                        VOP(eng, lambda hh=hh, h=h: eng.tensor_tensor(
                            out=sums[:, h:h + 1],
                            in0=sums[:, 8 + 2 * hh:9 + 2 * hh],
                            in1=sums[:, 9 + 2 * hh:10 + 2 * hh], op=ALU.add))
                        VOP(eng, lambda h=h: eng.reciprocal(
                            out=sums[:, h:h + 1], in_=sums[:, h:h + 1]))
                        VOP(eng, lambda hh=hh, h=h: eng.tensor_scalar(
                            out=esb[hh][:, 0:512], in0=esb[hh][:, 0:512],
                            scalar1=sums[:, h:h + 1], scalar2=None,
                            op0=ALU.mult))
                        VOP(eng, lambda hh=hh, h=h: eng.tensor_scalar(
                            out=esb[hh][:, 512:1024], in0=esb[hh][:, 512:1024],
                            scalar1=sums[:, h:h + 1], scalar2=None,
                            op0=ALU.mult))
                    rec[("scomb", b, iq, hp)] = chv.n

                for hh in range(2):
                    @blk.tensor
                    def _(eng, b=b, iq=iq, hp=hp, hh=hh):
                        eng.wait_ge(chv.h, rec[("scomb", b, iq, hp)])
                        for g2 in range(2):
                            if evt[0] > 0:
                                eng.wait_ge(tre.h, evt[0])
                            for bi in range(4):
                                c8 = g2 * 4 + bi
                                trc.inc(eng.transpose(
                                    pxtF[:, bi * 128:(bi + 1) * 128],
                                    esb[hh][:, c8 * 128:(c8 + 1) * 128],
                                    ident_sb[:, :]))
                            rec[("etr", b, iq, hp, hh, g2)] = trc.n
                            evt[0] += 1
                        rec[("etr_done", b, iq, hp, hh)] = trc.n

                    @blk.scalar
                    def _(eng, b=b, iq=iq, hp=hp, hh=hh):
                        for g2 in range(2):
                            eng.wait_ge(trc.h, rec[("etr", b, iq, hp, hh, g2)])
                            if g2 == 0:
                                prev = rec.get(
                                    ("av", b, iq, hp, hh - 1),
                                    rec.get(("av", b, iq, hp - 1, 1),
                                            rec.get(("av", b, iq - 1, 3, 1),
                                                    rec.get(("av", b - 1,
                                                             NQT2 - 1, 3, 1),
                                                            0))))
                                eng.wait_ge(mm2.h, prev)
                            tre.inc(eng.copy(
                                out=etf[:, g2 * 512:(g2 + 1) * 512],
                                in_=pxtF[:, :]))
                        rec[("etf", b, iq, hp, hh)] = tre.n

                    @blk.tensor
                    def _(eng, b=b, iq=iq, hp=hp, hh=hh, kst=kst):
                        eng.wait_ge(tre.h, rec[("etf", b, iq, hp, hh)])
                        eng.wait_ge(ev2.h, rec[("qkve", b, NKT - 1)])
                        prev = rec.get(("oev", b, iq, hp, hh - 1),
                                       rec.get(("oev", b, iq, hp - 1, 1),
                                               rec.get(("oev", b, iq - 1, 3,
                                                        1),
                                                       rec.get(("oev", b - 1,
                                                                NQT2 - 1, 3,
                                                                1), 0))))
                        eng.wait_ge(ev2.h, prev)
                        h = 2 * hp + hh
                        for c8 in range(8):
                            inst = eng.matmul(
                                pmAV[:, :],
                                vst[:, (kst + c8) * 512 + h * 64
                                    :(kst + c8) * 512 + (h + 1) * 64],
                                etf[:, c8 * 128:(c8 + 1) * 128],
                                start=(c8 == 0), stop=(c8 == 7))
                        mm2.inc(inst)
                        rec[("av", b, iq, hp, hh)] = mm2.n

                    @blk.scalar
                    def _(eng, b=b, iq=iq, hp=hp, hh=hh):
                        eng.wait_ge(mm2.h, rec[("av", b, iq, hp, hh)])
                        eng.wait_ge(mm2.h, rec.get(("womm", b, iq - 1),
                                                   rec.get(("womm", b - 1,
                                                            NQT2 - 1), 0)))
                        ev2.inc(eng.copy(
                            out=ofm[hh * 64:(hh + 1) * 64,
                                    hp * 128:(hp + 1) * 128],
                            in_=pmAV[:, :]))
                        rec[("oev", b, iq, hp, hh)] = ev2.n

            @blk.tensor
            def _(eng, b=b, iq=iq):
                eng.wait_ge(ev2.h, rec[("oev", b, iq, 3, 1)])
                for og in range(2):
                    for ic in range(4):
                        inst = eng.matmul(
                            pmS2[og][:, :],
                            ofm[:, ic * 128:(ic + 1) * 128],
                            wos[:, ic * 1024 + og * 512
                                :ic * 1024 + (og + 1) * 512],
                            start=(ic == 0), stop=(ic == 3))
                    mm2.inc(inst)
                rec[("womm", b, iq)] = mm2.n

            @blk.sync
            def _(eng, b=b, iq=iq):
                prev = rec.get(("xms", b, iq - 2),
                               rec.get(("xms", b - 1, 8 - (iq % 2)), 0))
                eng.wait_ge(xms[iq % 2].h, prev)
                if b == 0 and iq == 0:
                    eng.wait_ge(cns.h, wcv_rec[-1])
                xrl[iq % 2].inc(eng.dma_start(
                    out=xmb[iq % 2][:, :],
                    in_=xh[b, (KT0 + IQ0 + iq) * 128
                           :(KT0 + IQ0 + iq + 1) * 128, :]), 16)
                rec[("xrl", b, iq)] = xrl[iq % 2].n

            @blk.vector
            def _(eng, b=b, iq=iq):
                eng.wait_ge(mm2.h, rec[("womm", b, iq)])
                eng.wait_ge(xrl[iq % 2].h, rec[("xrl", b, iq)])
                for og in range(2):
                    VOP(eng, lambda og=og: eng.scalar_tensor_tensor(
                        out=xmb[iq % 2][:, og * 512:(og + 1) * 512],
                        in0=pmS2[og][:, :], scalar=1.0,
                        in1=xmb[iq % 2][:, og * 512:(og + 1) * 512],
                        op0=ALU.mult, op1=ALU.add))
                VOP(eng, lambda: eng.tensor_scalar(
                    out=xmb[iq % 2][:, :], in0=xmb[iq % 2][:, :],
                    scalar1=rzb[:, iq:iq + 1], scalar2=None, op0=ALU.mult))
                rec[("xme", b, iq)] = chv.n

            @blk.gpsimd
            def _(eng, b=b, iq=iq):
                eng.wait_ge(chv.h, rec[("xme", b, iq)])
                if b > 0:
                    eng.wait_ge(xml[0].h, rec[("xml_fin0", b - 1)])
                    eng.wait_ge(xml[1].h, rec[("xml_fin1", b - 1)])
                    eng.wait_ge(xle[0].h, rec[("xle_fin0", b - 1)])
                    eng.wait_ge(xle[1].h, rec[("xle_fin1", b - 1)])
                xms[iq % 2].inc(eng.dma_start(
                    out=xm[iq * 128:(iq + 1) * 128, :],
                    in_=xmb[iq % 2][:, :]), 16)
                rec[("xms", b, iq)] = xms[iq % 2].n

        for t in range(NOT_):
            slot = len(hb_hist)

            @blk.sync
            def _(eng, b=b, t=t, slot=slot):
                if slot >= 2:
                    eng.wait_ge(chv.h, hb_hist[slot - 2])
                eng.wait_ge(xms[(t + 1) % 2].h, rec[("xms", b, t + 1)])
                eng.wait_ge(xms[t % 2].h, rec[("xms", b, t)])
                xml[t % 2].inc(eng.dma_start(
                    out=hb[t % 2][:, 0:512],
                    in_=xm[(t + 1) * 128:(t + 2) * 128, 0:512]), 16)
                xml[t % 2].inc(eng.dma_start(
                    out=hb[t % 2][:, 512:1024],
                    in_=xm[(t + 1) * 128 - 1:(t + 2) * 128 - 1, 512:1024]), 16)
                rec[("xmlb", b, t)] = xml[t % 2].n

            @blk.vector
            def _(eng, b=b, t=t):
                eng.wait_ge(xml[t % 2].h, rec[("xmlb", b, t)])
                VOP(eng, lambda: eng.tensor_reduce(
                    out=statF[:, 24:25], in_=hb[t % 2][:, :], axis=AX.X,
                    op=ALU.add))
                VOP(eng, lambda: eng.tensor_scalar(
                    out=statF[:, 25:26], in0=statF[:, 24:25],
                    scalar1=1.0 / 1024.0, scalar2=None, op0=ALU.mult))
                VOP(eng, lambda: eng.tensor_scalar(
                    out=hb[t % 2][:, :], in0=hb[t % 2][:, :],
                    scalar1=statF[:, 25:26], scalar2=None, op0=ALU.subtract))
                rec[("lnc", b, t)] = chv.n

            @blk.scalar
            def _(eng, b=b, t=t):
                eng.wait_ge(chv.h, rec[("lnc", b, t)])
                SOP(eng, lambda: eng.activation(
                    out=hfm_sb[:, :], in_=hb[t % 2][:, :], func=AF.Square,
                    accum_out=statF[:, 26:27]))
                SOP(eng, lambda: eng.activation(
                    out=statF[:, 27:28], in_=statF[:, 26:27], func=AF.Sqrt,
                    scale=1.0 / 1024.0, bias=epsl[:, 0:1]))
                rec[("lns", b, t)] = chs.n

            @blk.vector
            def _(eng, b=b, t=t, slot=slot):
                eng.wait_ge(chs.h, rec[("lns", b, t)])
                VOP(eng, lambda: eng.reciprocal(out=statF[:, 28:29],
                                                in_=statF[:, 27:28]))
                VOP(eng, lambda: eng.tensor_scalar(
                    out=hb[t % 2][:, :], in0=hb[t % 2][:, :],
                    scalar1=statF[:, 28:29], scalar2=None, op0=ALU.mult))
                VOP(eng, lambda: eng.tensor_tensor(
                    out=hb[t % 2][:, :], in0=hb[t % 2][:, :], in1=lgb[:, :],
                    op=ALU.mult))
                VOP(eng, lambda: eng.tensor_tensor(
                    out=hb[t % 2][:, :], in0=hb[t % 2][:, :], in1=lbb[:, :],
                    op=ALU.add))
                if slot >= 2:
                    eng.wait_ge(trc.h, cst_hist[slot - 2])
                VOP(eng, lambda: eng.tensor_copy(out=cst[t % 2][:, :],
                                                 in_=hb[t % 2][:, :]))
                rec[("lncast", b, t)] = chv.n
                hb_hist.append(chv.n)

            @blk.tensor
            def _(eng, b=b, t=t):
                eng.wait_ge(chv.h, rec[("lncast", b, t)])
                for half in range(2):
                    if evt[0] > 0:
                        eng.wait_ge(tre.h, evt[0])
                    for bi in range(4):
                        trc.inc(eng.transpose(
                            pxtF[:, bi * 128:(bi + 1) * 128],
                            cst[t % 2][:, half * 512 + bi * 128
                                       :half * 512 + (bi + 1) * 128],
                            ident_sb[:, :]))
                    rec[("xotr", b, t, half)] = trc.n
                    evt[0] += 1
                cst_hist.append(trc.n)

            @blk.scalar
            def _(eng, b=b, t=t):
                for half in range(2):
                    eng.wait_ge(trc.h, rec[("xotr", b, t, half)])
                    if t == 0 and half == 0 and b > 0:
                        eng.wait_ge(mm2.h, rec[("w1mm_all", b - 1)])
                    tre.inc(eng.copy(
                        out=xfm_own[:, t * 1024 + half * 512
                                    :t * 1024 + (half + 1) * 512],
                        in_=pxtF[:, :]))
                rec[("xfme", b, t)] = tre.n
        rec[("xfma", b)] = rec[("xfme", b, NOT_ - 1)]

        for osi, osw in enumerate(W1OS):
            oc0 = sum(W1OS[:osi])
            a_hi = min(oc0 + osw, HID)
            g_lo = max(oc0, HID)

            @blk.sync
            def _(eng, b=b, osi=osi, oc0=oc0, osw=osw):
                prev = rec.get(("w1mm", b, osi - 2, NOT_ - 1),
                               rec.get(("w1mm", b - 1, NOS - 1, NOT_ - 1), 0))
                eng.wait_ge(mm2.h, prev)
                if b == 0 and osi < 2:
                    eng.wait_ge(bst.h, 16 * len(bspec))
                    eng.wait_ge(wst2[0].h, WST_ALL[0])
                    eng.wait_ge(wst2[1].h, WST_ALL[1])
                w1l[osi % 2].inc(eng.dma_start(
                    out=w1b[osi % 2][:, 0:8 * osw].rearrange(
                        "p (c o) -> p c o", c=8),
                    in_=w1c.rearrange("(c p) o -> p c o", p=128)[
                        :, :, oc0:oc0 + osw]), 16)
                w1l[osi % 2].inc(eng.dma_start(
                    out=brow[osi % 2][0:1, 0:osw],
                    in_=b1c[oc0:oc0 + osw].unsqueeze(0)), 16)
                rec[("w1l", b, osi)] = w1l[osi % 2].n

            def w1_prev(b2, o2, t2):
                if t2 >= 2:
                    return (b2, o2, t2 - 2)
                if o2 >= 1:
                    return (b2, o2 - 1, NOT_ - 2 + t2)
                if b2 >= 1:
                    return (b2 - 1, NOS - 1, NOT_ - 2 + t2)
                return None

            for t in range(NOT_):
                @blk.tensor
                def _(eng, b=b, osi=osi, osw=osw, t=t):
                    if t == 0:
                        eng.wait_ge(w1l[osi % 2].h, rec[("w1l", b, osi)])
                    if osi == 0 and t == 0:
                        eng.wait_ge(tre.h, rec[("xfma", b)])
                        eng.wait_ge(chv.h, rec[("xme", b, NQT2 - 1)])
                    prev = rec.get(("w1cp", b, osi, t - 2),
                                   rec.get(("w1cp", b, osi - 1, NOT_ - 2 + t),
                                           rec.get(("w1cp", b - 1, NOS - 1,
                                                    NOT_ - 2 + t), 0)))
                    eng.wait_ge(ev2.h, prev)
                    bank = pmS2[(t % 2) * 2]
                    for ic in range(8):
                        eng.matmul(bank[:, 0:osw],
                                   xfm_own[:, t * 1024 + ic * 128
                                           :t * 1024 + (ic + 1) * 128],
                                   w1b[osi % 2][:, ic * osw:(ic + 1) * osw],
                                   start=(ic == 0), stop=False)
                    inst = eng.matmul(bank[:, 0:osw], ones_bf[0:1, :],
                                      brow[osi % 2][0:1, 0:osw],
                                      start=False, stop=True)
                    mm2.inc(inst)
                    rec[("w1mm", b, osi, t)] = mm2.n

                @blk.scalar
                def _(eng, b=b, osi=osi, oc0=oc0, osw=osw, a_hi=a_hi,
                      g_lo=g_lo, w1_prev=w1_prev, t=t):
                    eng.wait_ge(mm2.h, rec[("w1mm", b, osi, t)])
                    pk = w1_prev(b, osi, t)
                    if pk is not None:
                        eng.wait_ge(ev2.h, rec[("w1ag",) + pk])
                        if (("w1evv",) + pk) in rec:
                            eng.wait_ge(chv.h, rec[("w1evv",) + pk])
                    bank = pmS2[(t % 2) * 2]
                    ev2.inc(eng.copy(out=gt[t % 2][:, 0:osw],
                                     in_=bank[:, 0:osw]))
                    rec[("w1cp", b, osi, t)] = ev2.n
                    if a_hi > oc0:
                        if osi == 0 and t == 0:
                            eng.wait_ge(mm2.h, rec[("av", b, NQT2 - 1, 3, 1)])
                            eng.wait_ge(chv.h, rec[("xme", b, NQT2 - 1)])
                        eng.wait_ge(ev2.h, ev2.n)
                        ev2.inc(eng.copy(
                            out=ab3[:, t, oc0:a_hi],
                            in_=gt[t % 2][:, 0:a_hi - oc0]))
                    if oc0 + osw > g_lo:
                        gw = oc0 + osw - g_lo
                        eng.wait_ge(ev2.h, ev2.n)
                        ev2.inc(eng.activation(
                            out=gtb[t % 2][:, 1024:1024 + gw],
                            in_=gt[t % 2][:, g_lo - oc0:g_lo - oc0 + gw],
                            func=AF.Gelu))
                    rec[("w1ag", b, osi, t)] = ev2.n

                if oc0 + osw > g_lo:
                    @blk.vector
                    def _(eng, b=b, osi=osi, oc0=oc0, osw=osw, g_lo=g_lo,
                          t=t):
                        eng.wait_ge(ev2.h, rec[("w1ag", b, osi, t)])
                        gw = oc0 + osw - g_lo
                        hc = g_lo - HID
                        VOP(eng, lambda: eng.tensor_tensor(
                            out=ab3[:, t, hc:hc + gw],
                            in0=gtb[t % 2][:, 1024:1024 + gw],
                            in1=ab3[:, t, hc:hc + gw], op=ALU.mult))
                        rec[("w1evv", b, osi, t)] = chv.n
        rec[("w1mm_all", b)] = rec[("w1mm", b, NOS - 1, NOT_ - 1)]

        @blk.sync
        def _(eng, b=b):
            prev = rec.get(("w2mm", b - 1, NOT_ - 1), 0)
            eng.wait_ge(mm2.h, prev)
            if b == 0:
                eng.wait_ge(bst.h, 16 * len(bspec))
            for og in range(2):
                ild.inc(eng.dma_start(
                    out=brow2[og][0:1, :],
                    in_=b2c[og * 512:(og + 1) * 512].unsqueeze(0)), 16)
            rec[("b2l", b)] = ild.n

        w2groups = [list(range(4 * g, min(4 * g + 4, NW2C))) for g in range(6)]
        for t in range(NOT_):
            gi = b * NOT_ + t

            @blk.tensor
            def _(eng, b=b, t=t):
                eng.wait_ge(chv.h, rec[("w1evv", b, NOS - 1, t)])
                eng.wait_ge(ev2.h, rec[("w1ag", b, NOS - 1, t)])
                for g, chunks in enumerate(w2groups):
                    if evt[0] > 0:
                        eng.wait_ge(tre.h, evt[0])
                    for j, c22 in enumerate(chunks):
                        wdt = 128 if c22 < NW2C - 1 else 42
                        trc.inc(eng.transpose(
                            pxtF[0:wdt, j * 128:(j + 1) * 128],
                            ab_all[:, t * HID + c22 * 128
                                   :t * HID + c22 * 128 + wdt],
                            ident_sb[:, :]))
                    rec[("htrg", b, t, g)] = trc.n
                    evt[0] += 1
                rec[("htr", b, t)] = trc.n

            @blk.scalar
            def _(eng, b=b, t=t):
                for g, chunks in enumerate(w2groups):
                    eng.wait_ge(trc.h, rec[("htrg", b, t, g)])
                    if g == 0:
                        prev = rec.get(("w2g", b, t - 1, 1),
                                       rec.get(("w2g", b - 1, NOT_ - 1, 1), 0))
                        eng.wait_ge(mm2.h, prev)
                    w = len(chunks) * 128
                    tre.inc(eng.copy(
                        out=hfm2[:, chunks[0] * 128:chunks[0] * 128 + w],
                        in_=pxtF[:, 0:w]))
                rec[("hfm2", b, t)] = tre.n

            for c22 in range(NW2C):
                wdt = 128 if c22 < NW2C - 1 else 42

                @blk.sync
                def _(eng, b=b, t=t, c22=c22, wdt=wdt):
                    prev = rec.get(("w2mmc", b, t, c22 - 2),
                                   rec.get(("w2mmc", b, t - 1,
                                            NW2C - 2 + c22),
                                           rec.get(("w2mmc", b - 1, NOT_ - 1,
                                                    NW2C - 2 + c22), 0)))
                    eng.wait_ge(mm2.h, prev)
                    if b == 0 and t == 0 and c22 < 2:
                        eng.wait_ge(wst2[0].h, WST_ALL[0])
                        eng.wait_ge(wst2[1].h, WST_ALL[1])
                    w2l[c22 % 2].inc(eng.dma_start(
                        out=w2b[c22 % 2][0:wdt, :],
                        in_=w2c[c22 * 128:c22 * 128 + wdt, :]), 16)
                    rec[("w2l", b, t, c22)] = w2l[c22 % 2].n

                @blk.tensor
                def _(eng, b=b, t=t, c22=c22, wdt=wdt):
                    if c22 == 0:
                        eng.wait_ge(tre.h, rec[("hfm2", b, t)])
                        eng.wait_ge(ild.h, rec[("b2l", b)])
                        if t == 0:
                            eng.wait_ge(ev2.h,
                                        rec[("w1ag", b, NOS - 1, NOT_ - 1)])
                        if rec.get(("yc", b, t - 1)) is not None:
                            eng.wait_ge(chv.h, rec[("yc", b, t - 1)])
                        elif b > 0 and t == 0:
                            eng.wait_ge(chv.h, rec[("yc", b - 1, NOT_ - 1)])
                    eng.wait_ge(w2l[c22 % 2].h, rec[("w2l", b, t, c22)])
                    eng.matmul(pmS2[2][:, :],
                               hfm2[0:wdt, c22 * 128:(c22 + 1) * 128],
                               w2b[c22 % 2][0:wdt, 0:512],
                               start=(c22 == 0), stop=False)
                    inst = eng.matmul(pmS2[3][:, :],
                                      hfm2[0:wdt, c22 * 128:(c22 + 1) * 128],
                                      w2b[c22 % 2][0:wdt, 512:1024],
                                      start=(c22 == 0), stop=False)
                    mm2.inc(inst)
                    rec[("w2mmc", b, t, c22)] = mm2.n

            @blk.tensor
            def _(eng, b=b, t=t):
                inst = eng.matmul(pmS2[2][:, :], ones_bf[0:1, :],
                                  brow2[0][0:1, :], start=False, stop=True)
                mm2.inc(inst)
                rec[("w2g", b, t, 0)] = mm2.n
                inst = eng.matmul(pmS2[3][:, :], ones_bf[0:1, :],
                                  brow2[1][0:1, :], start=False, stop=True)
                mm2.inc(inst)
                rec[("w2g", b, t, 1)] = mm2.n
                rec[("w2mm", b, t)] = mm2.n

            @blk.sync
            def _(eng, b=b, t=t, gi=gi):
                eng.wait_ge(yst[gi % 2].h, rec.get(("yst", gi - 2), 0))
                xle[t % 2].inc(eng.dma_start(
                    out=ybF[t % 2][:, :],
                    in_=xm[(t + 1) * 128:(t + 2) * 128, :]), 16)
                rec[("xleb", b, t)] = xle[t % 2].n
                if final:
                    if rec.get(("x1add", gi - 1)) is not None:
                        eng.wait_ge(chv.h, rec[("x1add", gi - 1)])
                    else:
                        eng.wait_ge(ev2.h, rec[("w1ag", b, NOS - 1, NOT_ - 2)])
                        eng.wait_ge(chv.h, rec[("w1evv", b, NOS - 1,
                                                NOT_ - 2)])
                    x1l.inc(eng.dma_start(
                        out=x1b2[:, :],
                        in_=x1o[b, t * 128:(t + 1) * 128, :]), 16)
                    rec[("x1lb", b, t)] = x1l.n

            @blk.vector
            def _(eng, b=b, t=t, gi=gi):
                eng.wait_ge(mm2.h, rec[("w2mm", b, t)])
                eng.wait_ge(xle[t % 2].h, rec[("xleb", b, t)])
                for og in range(2):
                    VOP(eng, lambda og=og: eng.scalar_tensor_tensor(
                        out=ybF[t % 2][:, og * 512:(og + 1) * 512],
                        in0=pmS2[2 + og][:, :], scalar=1.0,
                        in1=ybF[t % 2][:, og * 512:(og + 1) * 512],
                        op0=ALU.mult, op1=ALU.add))
                if final:
                    eng.wait_ge(x1l.h, rec[("x1lb", b, t)])
                    VOP(eng, lambda: eng.tensor_tensor(
                        out=ybF[t % 2][:, :], in0=ybF[t % 2][:, :],
                        in1=x1b2[:, :], op=ALU.add))
                    rec[("x1add", gi)] = chv.n
                    VOP(eng, lambda: eng.tensor_reduce(
                        out=statF[:, 24:25], in_=ybF[t % 2][:, :], axis=AX.X,
                        op=ALU.add))
                    VOP(eng, lambda: eng.tensor_scalar(
                        out=statF[:, 25:26], in0=statF[:, 24:25],
                        scalar1=1.0 / 1024.0, scalar2=None, op0=ALU.mult))
                    VOP(eng, lambda: eng.tensor_scalar(
                        out=ybF[t % 2][:, :], in0=ybF[t % 2][:, :],
                        scalar1=statF[:, 25:26], scalar2=None,
                        op0=ALU.subtract))
                rec[("yc", b, t)] = chv.n

            if final:
                @blk.scalar
                def _(eng, b=b, t=t):
                    eng.wait_ge(chv.h, rec[("yc", b, t)])
                    SOP(eng, lambda: eng.activation(
                        out=hfm_sb[:, :], in_=ybF[t % 2][:, :],
                        func=AF.Square, accum_out=statF[:, 29:30]))
                    SOP(eng, lambda: eng.activation(
                        out=statF[:, 30:31], in_=statF[:, 29:30],
                        func=AF.Sqrt, scale=1.0 / 1024.0, bias=epsl[:, 0:1]))
                    rec[("ys", b, t)] = chs.n

                @blk.vector
                def _(eng, b=b, t=t):
                    eng.wait_ge(chs.h, rec[("ys", b, t)])
                    VOP(eng, lambda: eng.reciprocal(out=statF[:, 31:32],
                                                    in_=statF[:, 30:31]))
                    VOP(eng, lambda: eng.tensor_scalar(
                        out=ybF[t % 2][:, :], in0=ybF[t % 2][:, :],
                        scalar1=statF[:, 31:32], scalar2=None, op0=ALU.mult))
                    rec[("yfin", b, t)] = chv.n
            else:
                rec[("yfin", b, t)] = rec[("yc", b, t)]

            @blk.gpsimd
            def _(eng, b=b, t=t, gi=gi):
                eng.wait_ge(chv.h, rec[("yfin", b, t)])
                yst[gi % 2].inc(eng.dma_start(
                    out=y_out[gi * 128:(gi + 1) * 128, :],
                    in_=ybF[t % 2][:, :]), 16)
                rec[("yst", gi)] = yst[gi % 2].n

        rec[("xml_fin0", b)] = xml[0].n
        rec[("xml_fin1", b)] = xml[1].n
        rec[("xle_fin0", b)] = xle[0].n
        rec[("xle_fin1", b)] = xle[1].n

    @blk.gpsimd
    def _(eng):
        eng.wait_ge(yst[0].h, yst[0].n)
        eng.wait_ge(yst[1].h, yst[1].n)

    p.close()
    return p


def _l0f_rope(c, kind):
    t = np.arange(2048, dtype=np.float64) + (c * 1024 - 1024)
    half = np.arange(0, 64, 2, dtype=np.float64) / 64.0
    invf = 10000.0 ** (-half)
    fr = t[:, None] * invf[None, :]
    base = (np.arange(0, 64, 2, dtype=np.float64) + 0.4 * 64) / (1.4 * 64)
    power = (t - 4096.0) / 256.0
    sc = base[None, :] ** power[:, None]
    cos = np.cos(fr)
    sin = np.sin(fr)
    if kind == "k":
        sc = 1.0 / sc
    cos64 = np.concatenate([cos * sc, cos * sc], 1)
    s1 = -sin * sc
    s2 = sin * sc
    out = np.zeros((2048, 1024), np.float32)
    out[:, 0:512] = np.tile(cos64, (1, 8)).astype(np.float32)
    out[:, 512:768] = np.tile(s1, (1, 8)).astype(np.float32)
    out[:, 768:1024] = np.tile(s2, (1, 8)).astype(np.float32)
    return out


def _l0f_mask(c):
    m = np.zeros((9, 128, 1024), np.float32)
    qq = np.arange(128)[:, None]
    kk = np.arange(1024)[None, :]
    for iq in range(9):
        P = c * 1024 - 128 + iq * 128
        K0 = (P // 512) * 512 - 512
        kpos = K0 + kk
        qpos = P + qq
        valid = (kpos >= 0) & (kpos <= qpos) & (kpos >= qpos - 512)
        m[iq] = np.where(valid, 0.0, -1e5)
    return m.astype(BF)


def _l0f_rz(c):
    rz = np.ones((128, 16), np.float32)
    for iq in range(9):
        pos = c * 1024 - 128 + iq * 128 + np.arange(128)
        rz[:, iq] = (pos >= 0).astype(np.float32)
    return rz


def _l0f_slices(xfull):
    out = np.zeros((NCORES, B, 2304, D), np.float32)
    for c in range(NCORES):
        s = c * 1024
        lo = s - 1280
        src_lo = max(lo, 0)
        out[c, :, src_lo - lo:src_lo - lo + (s + 1024 - src_lo), :] = \
            xfull[:, src_lo:s + 1024, :]
    return out


def run_l0f(inputs, x_bsd, layer, x1_bsd=None):
    final = x1_bsd is not None
    p = build_l0f(final=final)
    xs = _l0f_slices(x_bsd)
    idn = ident_np()
    maps = []
    for c in range(NCORES):
        m = {
            "xh": xs[c],
            "wqkv": inputs["L0_ffn_wqkv"][layer],
            "q_scale": inputs["L0_ffn_q_scale"][layer],
            "k_scale": inputs["L0_ffn_k_scale"][layer],
            "wo": inputs["L0_ffn_wo"][layer],
            "ln_g": inputs["L0_ffn_ln_g"][layer],
            "ln_b": inputs["L0_ffn_ln_b"][layer],
            "w1": inputs["L0_ffn_w1"][layer],
            "b1": inputs["L0_ffn_b1"][layer],
            "w2": inputs["L0_ffn_w2"][layer],
            "b2": inputs["L0_ffn_b2"][layer],
            "rq": _l0f_rope(c, "q"),
            "rk": _l0f_rope(c, "k"),
            "msk": _l0f_mask(c),
            "rz": _l0f_rz(c),
            "ident": idn,
        }
        if final:
            m["x1o"] = np.ascontiguousarray(
                x1_bsd[:, c * 1024:(c + 1) * 1024, :])
        maps.append(m)
    res = run_bass_kernel_spmd(p.nc, maps, core_ids=list(range(NCORES)))
    out = np.empty((B, S, D), np.float32)
    for c in range(NCORES):
        out[:, c * 1024:(c + 1) * 1024, :] = \
            np.asarray(res.results[c]["y"]).reshape(B, 1024, D)
    return out


def kernel(**inputs):
    x = np.ascontiguousarray(np.asarray(inputs["L0_attn_x"], np.float32))
    x1 = run_l0a(inputs, x)
    x2a = run_l0f(inputs, x1, 0)
    x2 = run_l0f(inputs, x2a, 1, x1_bsd=x1)
    x3 = run_l1a(inputs, x2)
    x4 = run_l1f(inputs, x3)
    return x4

